# version 08
# baseline (speedup 1.0000x reference)
import contextlib
import numpy as np
import concourse.bass as bass
import concourse.mybir as mybir
from concourse.bass_utils import run_bass_kernel_spmd

F32 = mybir.dt.float32
BF16 = mybir.dt.bfloat16
AF = mybir.ActivationFunctionType
ALU = mybir.AluOpType
AX = mybir.AxisListType

NCORE = 8
L = 16384
D = 1024
DI = 2048
NS = 16
T = L // NCORE
TB = 512
NB = T // TB
Q = 128
NQ = T // Q
QPB = TB // Q
DC = D // 128
CC = DI // 128
NE = 8
HK = 16
HALO = 3
XW = HALO + TB
EPS = 1e-5
BIG = 1e30

P3_ON = True
P4_ON = True
P5_ON = True
P6_ON = True
DEBUG = False


class _Op:
    __slots__ = ("eng", "fn", "after", "kind", "inc", "val", "label", "lane")

    def __init__(self, eng, fn, after, kind, label):
        self.eng, self.fn, self.after, self.kind = eng, fn, after, kind
        self.inc = False
        self.val = None
        self.label = label
        self.lane = 0


class Builder:
    ENGINES = ("pe", "dv", "ac", "pl", "sy")
    NLANES = 8

    def __init__(self):
        self.streams = {e: [] for e in self.ENGINES}
        self.dmacnt = {e: 0 for e in self.ENGINES}

    def do(self, eng, fn, after=(), kind="compute", label=""):
        seen, flt = set(), []
        for a in after:
            if a is None or id(a) in seen:
                continue
            seen.add(id(a))
            flt.append(a)
        op = _Op(eng, fn, flt, kind, label)
        if kind == "dma":
            op.lane = self.dmacnt[eng] % self.NLANES
            self.dmacnt[eng] += 1
            op.inc = True
        for a in flt:
            if a.eng == "pe" and eng == "pe" and a.kind == "compute" \
                    and kind == "compute":
                continue
            a.inc = True
        self.streams[eng].append(op)
        return op

    def finalize_and_emit(self, nc):
        self.sems = {}
        for e in self.ENGINES:
            self.sems[("c", e)] = nc.semaphore(f"sc_{e}").__enter__()
            self.sems[("cc", e)] = nc.semaphore(f"scc_{e}").__enter__()
            for ln in range(self.NLANES):
                self.sems[("d", e, ln)] = nc.semaphore(
                    f"sd_{e}_{ln}").__enter__()
        cnt = {k: 0 for k in self.sems}
        for e in self.ENGINES:
            for op_ in self.streams[e]:
                if op_.inc:
                    k = self._semkey(op_)
                    cnt[k] += self._amt(op_)
                    op_.val = cnt[k]
        block = nc.Block().__enter__()
        regs = {"pe": block.tensor, "dv": block.vector, "ac": block.scalar,
                "pl": block.gpsimd, "sy": block.sync}
        for ename in self.ENGINES:
            stream = self.streams[ename]
            if not stream:
                continue

            def emit_stream(eng, stream=stream, ename=ename):
                clock = {}
                for op_ in stream:
                    for a in op_.after:
                        if (a.eng == "pe" and ename == "pe"
                                and a.kind == "compute"
                                and op_.kind == "compute"):
                            continue
                        k = self._semkey(a)
                        if clock.get(k, 0) < a.val:
                            eng.wait_ge(self.sems[k], a.val)
                            clock[k] = a.val
                    if op_.kind == "dma" and op_.val > 16:
                        k = self._semkey(op_)
                        if clock.get(k, 0) < op_.val - 16:
                            eng.wait_ge(self.sems[k], op_.val - 16)
                            clock[k] = op_.val - 16
                    inst = op_.fn(eng) if op_.fn is not None else None
                    if op_.inc:
                        inst.then_inc(self.sems[self._semkey(op_)],
                                      self._amt(op_))

            regs[ename](emit_stream)
        block.__exit__(None, None, None)

    @staticmethod
    def _semkey(op_):
        if op_.kind == "dma":
            return ("d", op_.eng, op_.lane)
        if op_.kind == "cc":
            return ("cc", op_.eng)
        return ("c", op_.eng)

    @staticmethod
    def _amt(op_):
        return 16 if op_.kind == "dma" else 1


class Res:
    __slots__ = ("w", "r")

    def __init__(self, barrier=()):
        self.w = None
        self.r = list(barrier)


def build_nc():
    nc = bass.Bass("TRN2", target_bir_lowering=False)
    b = Builder()
    RES = {}

    def op(eng, fn, r=(), w=(), extra=(), kind="compute", label=""):
        deps = list(extra)
        for x in r:
            deps.append(x.w)
        for x in w:
            deps.append(x.w)
            deps.extend(x.r)
        t = b.do(eng, fn, after=deps, kind=kind, label=label)
        for x in r:
            x.r.append(t)
        for x in w:
            x.w = t
            x.r = []
        return t

    def R(name):
        return RES[name]

    def din(name, shape, dtype=F32):
        return nc.dram_tensor(name, list(shape), dtype,
                              kind="ExternalInput").ap()

    xt = din("xt", (D, HALO + T))
    ipwt_x = din("ipwt_x", (D, DI))
    ipwt_z = din("ipwt_z", (D, DI))
    xpwt = din("xpwt", (DI, 80))
    opwt = din("opwt", (DI, D))
    w12t = din("w12t", (NE, D, 2 * DI))
    w3t = din("w3t", (NE, DI, D))
    gwt = din("gwt", (D, NE))
    convw_i = din("convw_p", (128, CC * 4))
    convb_i = din("convb_p", (128, CC))
    drow_i = din("drow", (1, DI))
    alog_i = din("alog", (1, NS))
    ones2d_i = din("ones2d", (128, 128))
    identf_i = din("identf_h", (128, 128))
    masktri_i = din("masktri_h", (128, 128))
    selbc_i = din("selbc_h", (NS, NE))
    gmaskt_i = din("gmaskt_h", (NE, NE))

    outt = nc.dram_tensor("outt", [D, T], F32, kind="ExternalOutput").ap()

    dk = dict(kind="ExternalOutput") if DEBUG else {}
    xn_d = nc.dram_tensor("xn_d", [128, DC * T], BF16, **dk).ap()
    xct_d = nc.dram_tensor("xct_d", [NQ * 128, DI], BF16, **dk).ap()
    hst_d = nc.dram_tensor("hst_d", [NQ * NS, DI], F32, **dk).ap()
    prow_d = nc.dram_tensor("prow_d", [NE, T], F32, **dk).ap()
    ccin = nc.dram_tensor("ccin", [NS + 1, DI], F32).ap()
    ccout = nc.dram_tensor("ccout", [NCORE * (NS + 1), DI], F32).ap()
    for n_ in ("xn_d", "xct_d", "hst_d", "prow_d", "ccin", "ccout", "yfin"):
        RES[n_] = Res()

    layers = {}

    def sb(name, p, wdt, dt=F32, layer="perm", barrier=()):
        if layer not in layers:
            layers[layer] = contextlib.ExitStack()
        uname = name if layer == "perm" else f"{name}__{layer}"
        t = layers[layer].enter_context(nc.sbuf_tensor(uname, [p, wdt], dt))
        RES[name] = Res(barrier)
        return t

    def retire(*names):
        toks = []
        for n_ in names:
            res = RES.pop(n_)
            if res.w is not None:
                toks.append(res.w)
            toks.extend(res.r)
        return toks

    def close_layer(layer):
        layers.pop(layer).close()

    y = sb("y", 128, DC * T)
    ones_f = sb("ones_f", 128, 128)
    ones_b = sb("ones_b", 128, 128, BF16)
    identb = sb("identb", 128, 128, BF16)
    identf = sb("identf", 128, 128)
    masktri = sb("masktri", 128, 128)
    convb_s = sb("convb_s", 128, CC)
    gw_s = sb("gw_s", 128, DC * NE, BF16)
    abc_s = sb("abc_s", 128, NS)
    a_col = sb("a_col", NS, 1)
    rows16 = sb("rows16", 1, 64)
    srow8 = sb("srow8", 1, 8)
    scol8 = sb("scol8", NE, 1)
    g8 = sb("g8", NE, 1)
    Dt16 = sb("Dt16", NS, NS)
    Pt16 = sb("Pt16", NS, NS)
    coef16 = sb("coef16", NS, NE)
    selbc = sb("selbc", NS, NE)
    gmaskt = sb("gmaskt", NE, NE)
    xnh = sb("xnh", 128, DC * HALO, BF16)
    cst2 = sb("cst2", 1, 2)
    rows_b = sb("rows_b", 1, 1024, BF16)
    xnblk = sb("xnblk", 128, DC * TB, BF16)
    xnblk2 = sb("xnblk2", 128, DC * TB, BF16)

    def pst(name, p, wdt, dt=F32, layer="psum", barrier=()):
        if layer not in layers:
            layers[layer] = contextlib.ExitStack()
        t = layers[layer].enter_context(nc.psum_tensor(name, [p, wdt], dt))
        RES[name] = Res(barrier)
        return t

    pA = pst("pA", 128, 512)
    pB = pst("pB", 128, 512)
    pD = pst("pD", 128, 512)
    pS = pst("pS", 128, 512)
    pBc = pst("pBc", 128, 512)
    pBc2 = pst("pBc2", 128, 512)
    pTrA = pst("pTrA", 128, 512, BF16, layer="psumT")
    pTrB = pst("pTrB", 128, 512, BF16, layer="psumT")

    def dma(queue, dst, src, r=(), w=(), label="dma"):
        return op(queue, lambda e, d=dst, s=src: e.dma_start(out=d, in_=s),
                  r=r, w=w, kind="dma", label=label)

    def mm(out_ap, lhsT_ap, rhs_ap, start, stop, r=(), w=(), label="mm"):
        return op("pe", lambda e: e.matmul(out_ap, lhsT_ap, rhs_ap,
                                           start=start, stop=stop),
                  r=r, w=w, label=label)

    def act(out_ap, in_ap, func, r=(), w=(), scale=None, bias=None,
            label="act"):
        kw = {}
        if scale is not None:
            kw["scale"] = scale
        if bias is not None:
            kw["bias"] = bias
        return op("ac", lambda e: e.activation(out_ap, in_ap, func, **kw),
                  r=r, w=w, label=label)

    dma("sy", ones_f[:, :], ones2d_i, w=[R("ones_f")])
    dma("pl", ones_b[:, :], ones2d_i, w=[R("ones_b")])
    dma("pl", identb[:, :], identf_i, w=[R("identb")])
    dma("sy", identf[:, :], identf_i, w=[R("identf")])
    dma("sy", masktri[:, :], masktri_i, w=[R("masktri")])
    dma("sy", convb_s[:, :], convb_i, w=[R("convb_s")])
    dma("pl", gw_s[:, :].rearrange("p (dc e) -> p dc e", e=NE),
        gwt.rearrange("(dc p) e -> p dc e", p=128), w=[R("gw_s")])
    dma("sy", selbc[:, :], selbc_i, w=[R("selbc")])
    dma("sy", gmaskt[:, :], gmaskt_i, w=[R("gmaskt")])
    dma("sy", rows16[0:1, 48:64], alog_i, w=[R("rows16")])

    act(rows16[0:1, 32:48], rows16[0:1, 48:64], AF.Exp,
        r=[R("rows16")], w=[R("rows16")], label="a_exp")
    op("dv", lambda e: e.tensor_scalar_mul(rows16[0:1, 32:48],
                                           rows16[0:1, 32:48], -1.0),
       r=[R("rows16")], w=[R("rows16")], label="a_neg")
    op("dv", lambda e: e.memset(cst2[0:1, 0:1], EPS), w=[R("cst2")])
    op("dv", lambda e: e.memset(cst2[0:1, 1:2], 1.0), w=[R("cst2")])
    dma("pl", a_col[:, :], rows16[0:1, 32:48], r=[R("rows16")],
        w=[R("a_col")], label="a_col")
    mm(pS[:, 0:NS], ones_f[0:1, :], rows16[0:1, 32:48], True, True,
       r=[R("ones_f"), R("rows16")], w=[R("pS")], label="a_bc")
    op("dv", lambda e: e.tensor_copy(abc_s[:, :], pS[:, 0:NS]),
       r=[R("pS")], w=[R("abc_s")], label="a_bcev")

    yv = y[:, :].rearrange("p (dc t) -> p dc t", t=T)
    xnv_d = xn_d[:, :].rearrange("p (dc t) -> p dc t", t=T)
    xnbufs = [(xnblk, R("xnblk")), (xnblk2, R("xnblk2"))]

    for blk in range(NB):
        src = xt[:, HALO + blk * TB: HALO + (blk + 1) * TB]
        src = src.rearrange("(dc p) t -> p dc t", p=128)
        dma("sy", yv[:, :, blk * TB:(blk + 1) * TB], src,
            w=[R("y")], label=f"xld{blk}")

    xb_bf = sb("xb_bf", 128, DC * TB, BF16, layer="E0")
    sq_bf = sb("sq_bf", 128, DC * TB, BF16, layer="E0")
    sc_all = sb("sc_all", NE, T, layer="E0")
    pe8T_all = sb("pe8T_all", NE, T, layer="E0")
    rowsA = sb("rowsA", 1, 4 * TB, layer="E0")
    acc_f = sb("acc_f", 128, TB, layer="E0")
    acc_f2 = sb("acc_f2", 128, TB, layer="E0")
    m1_all = sb("m1_all", 128, NQ, layer="E0")
    m2_all = sb("m2_all", 128, NQ, layer="E0")
    ge8 = sb("ge8", 128, 2 * NE, layer="E0")
    p12 = sb("p12", 128, 2 * NQ, layer="E0")
    s8_all = sb("s8_all", 128, NQ * NE, layer="E0")

    dma("sy", acc_f[:, 0:DC * HALO].rearrange("p (dc t) -> p dc t", t=HALO),
        xt[:, 0:HALO].rearrange("(dc p) t -> p dc t", p=128),
        w=[R("acc_f")], label="haloload")
    op("dv", lambda e: e.tensor_copy(xb_bf[:, 0:DC * HALO],
                                     acc_f[:, 0:DC * HALO]),
       r=[R("acc_f")], w=[R("xb_bf")], label="halocast")
    act(sq_bf[:, 0:DC * HALO], xb_bf[:, 0:DC * HALO], AF.Square,
        r=[R("xb_bf")], w=[R("sq_bf")], label="halosq")
    for dc in range(DC):
        mm(pS[0:1, 0:HALO], ones_b[:, 0:1],
           xb_bf[:, dc * HALO:(dc + 1) * HALO], dc == 0, dc == DC - 1,
           r=[R("ones_b"), R("xb_bf")], w=[R("pS")], label=f"hm{dc}")
    for dc in range(DC):
        mm(pBc2[0:1, 0:HALO], ones_b[:, 0:1],
           sq_bf[:, dc * HALO:(dc + 1) * HALO], dc == 0, dc == DC - 1,
           r=[R("ones_b"), R("sq_bf")], w=[R("pBc2")], label=f"hv{dc}")
    op("dv", lambda e: e.tensor_scalar_mul(rowsA[0:1, 0:HALO],
                                           pS[0:1, 0:HALO], 1.0 / D),
       r=[R("pS")], w=[R("rowsA")], label="hm2")
    op("dv", lambda e: e.tensor_scalar_mul(rowsA[0:1, 8:8 + HALO],
                                           pBc2[0:1, 0:HALO], 1.0 / D),
       r=[R("pBc2")], w=[R("rowsA")], label="he2")
    op("dv", lambda e: e.tensor_mul(rowsA[0:1, 16:16 + HALO],
                                    rowsA[0:1, 0:HALO],
                                    rowsA[0:1, 0:HALO]),
       r=[R("rowsA")], w=[R("rowsA")], label="hmsq")
    op("dv", lambda e: e.tensor_sub(rowsA[0:1, 8:8 + HALO],
                                    rowsA[0:1, 8:8 + HALO],
                                    rowsA[0:1, 16:16 + HALO]),
       r=[R("rowsA")], w=[R("rowsA")], label="hvar")
    act(rowsA[0:1, 8:8 + HALO], rowsA[0:1, 8:8 + HALO],
        AF.Ln, bias=cst2[0:1, 0:1], r=[R("rowsA"), R("cst2")],
        w=[R("rowsA")], label="hln")
    act(rowsA[0:1, 8:8 + HALO], rowsA[0:1, 8:8 + HALO],
        AF.Exp, scale=-0.5, r=[R("rowsA")], w=[R("rowsA")], label="hrstd")
    op("dv", lambda e: e.tensor_copy(rows_b[0:1, 0:HALO],
                                     rowsA[0:1, 0:HALO]),
       r=[R("rowsA")], w=[R("rows_b")], label="hmbf")
    op("dv", lambda e: e.tensor_copy(rows_b[0:1, 8:8 + HALO],
                                     rowsA[0:1, 8:8 + HALO]),
       r=[R("rowsA")], w=[R("rows_b")], label="hrbf")
    mm(pBc[:, 0:HALO], ones_b[0:1, :], rows_b[0:1, 0:HALO], True, True,
       r=[R("ones_b"), R("rows_b")], w=[R("pBc")], label="hmbc")
    mm(pBc2[:, 0:HALO], ones_b[0:1, :], rows_b[0:1, 8:8 + HALO],
       True, True, r=[R("ones_b"), R("rows_b")], w=[R("pBc2")], label="hrbc")
    for dc in range(DC):
        op("dv", lambda e, dc=dc: e.tensor_sub(
            acc_f2[:, dc * HALO:(dc + 1) * HALO],
            acc_f[:, dc * HALO:(dc + 1) * HALO], pBc[:, 0:HALO]),
            r=[R("acc_f"), R("pBc")], w=[R("acc_f2")], label=f"hxs{dc}")
        op("dv", lambda e, dc=dc: e.tensor_mul(
            xnh[:, dc * HALO:(dc + 1) * HALO],
            acc_f2[:, dc * HALO:(dc + 1) * HALO], pBc2[:, 0:HALO]),
            r=[R("acc_f2"), R("pBc2")], w=[R("xnh")], label=f"hxn{dc}")

    for blk in range(NB):
        xnb_t, xnb_r = xnbufs[blk % 2]
        for dc in range(DC):
            op("dv", lambda e, dc=dc, blk=blk: e.tensor_copy(
                xb_bf[:, dc * TB:(dc + 1) * TB],
                y[:, dc * T + blk * TB: dc * T + (blk + 1) * TB]),
                r=[R("y")], w=[R("xb_bf")], label=f"cast{blk}_{dc}")
        for dc in range(DC):
            act(sq_bf[:, dc * TB:(dc + 1) * TB],
                xb_bf[:, dc * TB:(dc + 1) * TB], AF.Square,
                r=[R("xb_bf")], w=[R("sq_bf")], label=f"sq{blk}_{dc}")
        for dc in range(DC):
            mm(pS[0:1, 0:512], ones_b[:, 0:1],
               xb_bf[:, dc * TB:(dc + 1) * TB], dc == 0, dc == DC - 1,
               r=[R("ones_b"), R("xb_bf")], w=[R("pS")], label=f"ms{blk}_{dc}")
        for dc in range(DC):
            mm(pBc2[0:1, 0:512], ones_b[:, 0:1],
               sq_bf[:, dc * TB:(dc + 1) * TB], dc == 0, dc == DC - 1,
               r=[R("ones_b"), R("sq_bf")], w=[R("pBc2")],
               label=f"vs{blk}_{dc}")
        op("dv", lambda e: e.tensor_scalar_mul(rowsA[0:1, 0:TB],
                                               pS[0:1, 0:512], 1.0 / D),
           r=[R("pS")], w=[R("rowsA")], label=f"m{blk}")
        op("dv", lambda e: e.tensor_scalar_mul(rowsA[0:1, TB:2 * TB],
                                               pBc2[0:1, 0:512], 1.0 / D),
           r=[R("pBc2")], w=[R("rowsA")], label=f"e2{blk}")
        op("dv", lambda e: e.tensor_mul(rowsA[0:1, 2 * TB:3 * TB],
                                        rowsA[0:1, 0:TB], rowsA[0:1, 0:TB]),
           r=[R("rowsA")], w=[R("rowsA")], label=f"msq{blk}")
        op("dv", lambda e: e.tensor_sub(rowsA[0:1, TB:2 * TB],
                                        rowsA[0:1, TB:2 * TB],
                                        rowsA[0:1, 2 * TB:3 * TB]),
           r=[R("rowsA")], w=[R("rowsA")], label=f"var{blk}")
        act(rowsA[0:1, TB:2 * TB], rowsA[0:1, TB:2 * TB], AF.Ln,
            bias=cst2[0:1, 0:1], r=[R("rowsA"), R("cst2")], w=[R("rowsA")],
            label=f"lnv{blk}")
        act(rowsA[0:1, TB:2 * TB], rowsA[0:1, TB:2 * TB], AF.Exp, scale=-0.5,
            r=[R("rowsA")], w=[R("rowsA")], label=f"rstd{blk}")
        op("dv", lambda e: e.tensor_copy(rows_b[0:1, 0:512],
                                         rowsA[0:1, 0:TB]),
           r=[R("rowsA")], w=[R("rows_b")], label=f"mbf{blk}")
        op("dv", lambda e: e.tensor_copy(rows_b[0:1, 512:1024],
                                         rowsA[0:1, TB:2 * TB]),
           r=[R("rowsA")], w=[R("rows_b")], label=f"rbf{blk}")
        mm(pBc[:, :], ones_b[0:1, :], rows_b[0:1, 0:512], True, True,
           r=[R("ones_b"), R("rows_b")], w=[R("pBc")], label=f"mbc{blk}")
        mm(pBc2[:, :], ones_b[0:1, :], rows_b[0:1, 512:1024], True, True,
           r=[R("ones_b"), R("rows_b")], w=[R("pBc2")], label=f"rbc{blk}")
        accbufs = [(acc_f, R("acc_f")), (acc_f2, R("acc_f2"))]
        for dc in range(DC):
            ab, ar = accbufs[dc % 2]
            op("dv", lambda e, dc=dc, blk=blk, ab=ab: e.tensor_sub(
                ab[:, :], y[:, dc * T + blk * TB: dc * T + (blk + 1) * TB],
                pBc[:, :]),
                r=[R("y"), R("pBc")], w=[ar], label=f"xs{blk}_{dc}")
            op("dv", lambda e, dc=dc, ab=ab, xnb_t=xnb_t: e.tensor_mul(
                xnb_t[:, dc * TB:(dc + 1) * TB], ab[:, :], pBc2[:, :]),
                r=[ar, R("pBc2")], w=[xnb_r], label=f"xn{blk}_{dc}")
        dma("sy", xnv_d[:, :, blk * TB:(blk + 1) * TB],
            xnb_t[:, :].rearrange("p (dc t) -> p dc t", t=TB),
            r=[xnb_r], w=[R("xn_d")], label=f"xnst{blk}")
        for dc in range(DC):
            mm(pD[0:NE, 0:512], gw_s[:, dc * NE:(dc + 1) * NE],
               xnb_t[:, dc * TB:(dc + 1) * TB], dc == 0, dc == DC - 1,
               r=[R("gw_s"), xnb_r], w=[R("pD")], label=f"gsc{blk}_{dc}")
        op("dv", lambda e, blk=blk: e.tensor_copy(
            sc_all[:, blk * TB:(blk + 1) * TB], pD[0:NE, 0:512]),
            r=[R("pD")], w=[R("sc_all")], label=f"gscev{blk}")

    for q in range(NQ):
        op("pe", lambda e, q=q: e.transpose(
            pBc[:, 0:NE], sc_all[:, q * 128:(q + 1) * 128],
            identf[0:NE, 0:NE]),
           r=[R("sc_all"), R("identf")], w=[R("pBc")], label=f"scT{q}")
        op("dv", lambda e, q=q: e.tensor_copy(
            s8_all[:, q * NE:(q + 1) * NE], pBc[:, 0:NE]),
           r=[R("pBc")], w=[R("s8_all")], label=f"scTe{q}")
    for q in range(NQ):
        s8 = s8_all[:, q * NE:(q + 1) * NE]
        op("dv", lambda e, s8=s8, q=q: e.reduce_max(
            m1_all[:, q:q + 1], s8, axis=AX.X),
           r=[R("s8_all")], w=[R("m1_all")], label=f"m1_{q}")
        op("dv", lambda e, s8=s8, q=q: e.tensor_scalar(
            ge8[:, 0:NE], s8, m1_all[:, q:q + 1], None, ALU.is_ge),
           r=[R("s8_all"), R("m1_all")], w=[R("ge8")], label=f"ge1_{q}")
        op("dv", lambda e, s8=s8: e.scalar_tensor_tensor(
            ge8[:, NE:2 * NE], ge8[:, 0:NE], -BIG, s8, ALU.mult, ALU.add),
           r=[R("s8_all"), R("ge8")], w=[R("ge8")], label=f"sm_{q}")
        op("dv", lambda e, q=q: e.reduce_max(
            m2_all[:, q:q + 1], ge8[:, NE:2 * NE], axis=AX.X),
           r=[R("ge8")], w=[R("m2_all")], label=f"m2_{q}")
        op("dv", lambda e, s8=s8, q=q: e.tensor_scalar(
            ge8[:, NE:2 * NE], s8, m2_all[:, q:q + 1], None, ALU.is_ge),
           r=[R("s8_all"), R("m2_all")], w=[R("ge8")], label=f"ge2_{q}")
        op("dv", lambda e: e.tensor_sub(
            ge8[:, NE:2 * NE], ge8[:, NE:2 * NE], ge8[:, 0:NE]),
           r=[R("ge8")], w=[R("ge8")], label=f"eq2_{q}")
        op("dv", lambda e, q=q: e.tensor_sub(
            p12[:, q:q + 1], m1_all[:, q:q + 1], m2_all[:, q:q + 1]),
           r=[R("m1_all"), R("m2_all")], w=[R("p12")], label=f"d12_{q}")
        act(p12[:, q:q + 1], p12[:, q:q + 1], AF.Sigmoid,
            r=[R("p12")], w=[R("p12")], label=f"p1_{q}")
        op("dv", lambda e, q=q: e.tensor_scalar(
            p12[:, NQ + q:NQ + q + 1], p12[:, q:q + 1], -1.0, 1.0,
            ALU.mult, ALU.add),
           r=[R("p12")], w=[R("p12")], label=f"p2_{q}")
        op("dv", lambda e, q=q: e.tensor_scalar(
            ge8[:, 0:NE], ge8[:, 0:NE], p12[:, q:q + 1], None, ALU.mult),
           r=[R("ge8"), R("p12")], w=[R("ge8")], label=f"u_{q}")
        op("dv", lambda e, s8=s8, q=q: e.scalar_tensor_tensor(
            s8, ge8[:, NE:2 * NE], p12[:, NQ + q:NQ + q + 1], ge8[:, 0:NE],
            ALU.mult, ALU.add),
           r=[R("ge8"), R("p12")], w=[R("s8_all")], label=f"pe8_{q}")
        op("pe", lambda e, s8=s8, q=q: e.transpose(
            pBc2[0:NE, 0:128], s8, identf[:, :]),
           r=[R("s8_all"), R("identf")], w=[R("pBc2")], label=f"pT{q}")
        op("dv", lambda e, q=q: e.tensor_copy(
            pe8T_all[:, q * 128:(q + 1) * 128], pBc2[0:NE, 0:128]),
           r=[R("pBc2")], w=[R("pe8T_all")], label=f"pTe{q}")
    dma("sy", prow_d, pe8T_all[:, :], r=[R("pe8T_all")], w=[R("prow_d")],
        label="prowst")

    bar0 = retire("xb_bf", "sq_bf", "sc_all", "pe8T_all", "rowsA",
                  "acc_f", "acc_f2", "m1_all", "m2_all", "ge8", "p12",
                  "s8_all")
    close_layer("E0")
    bar1a = bar0

    if P3_ON:
        call = sb("call", NS, T, layer="E1s", barrier=bar0)
        lhsT2 = sb("lhsT2", 2, T, layer="E1s", barrier=bar0)
        rhs2 = sb("rhs2", 2, T, layer="E1s", barrier=bar0)
        qt_all = sb("qt_all", NS, T, BF16, layer="E1s", barrier=bar0)
        wstr_a = sb("wstr_a", 128, DC * TB, BF16, layer="E1s", barrier=bar0)
        wstr_b = sb("wstr_b", 128, DC * TB, BF16, layer="E1s", barrier=bar0)
        xcT_s = sb("xcT_s", 128, DI, BF16, layer="E1s", barrier=bar0)
        xcT_s2 = sb("xcT_s2", 128, DI, BF16, layer="E1s", barrier=bar0)
        btT_all = sb("btT_all", 128, NQ * NS, layer="E1s", barrier=bar0)
        khT_all = sb("khT_all", 128, NQ * NS, BF16, layer="E1s",
                     barrier=bar0)
        dsm = sb("dsm", 128, 128, layer="E1s", barrier=bar0)
        dsm2 = sb("dsm2", 128, 128, layer="E1s", barrier=bar0)
        en_a = sb("en_a", 128, 128, layer="E1s", barrier=bar0)
        en_b = sb("en_b", 128, 128, layer="E1s", barrier=bar0)
        en_c = sb("en_c", 128, 128, layer="E1s", barrier=bar0)
        en_d = sb("en_d", 128, 128, layer="E1s", barrier=bar0)
        mt_f = sb("mt_f", 128, 128, layer="E1s", barrier=bar0)
        mt_f2 = sb("mt_f2", 128, 128, layer="E1s", barrier=bar0)
        mt_t = sb("mt_t", 128, 128, layer="E1s", barrier=bar0)
        mt_t2 = sb("mt_t2", 128, 128, layer="E1s", barrier=bar0)
        mtb = sb("mtb", 128, 128, BF16, layer="E1s", barrier=bar0)
        mtb2 = sb("mtb2", 128, 128, BF16, layer="E1s", barrier=bar0)
        delta_row = sb("delta_row", 1, T, layer="E1c", barrier=bar0)
        ball = sb("ball", NS, T, layer="E1c", barrier=bar0)
        xi_blk = sb("xi_blk", 128, CC * XW, BF16, layer="E1a", barrier=bar0)
        for cc in range(CC):
            RES[f"xi{cc}"] = Res(bar0)
        xcT_w = sb("xcT_w", 128, DI, BF16, layer="E1a", barrier=bar0)
        xcT_w2 = sb("xcT_w2", 128, DI, BF16, layer="E1a", barrier=bar0)
        convw_s = sb("convw_s", 128, CC * 4, layer="E1a", barrier=bar0)
        xpw_s = sb("xpw_s", 128, CC * 80, BF16, layer="E1a", barrier=bar0)
        cacc = sb("cacc", 128, TB, BF16, layer="E1a", barrier=bar0)
        cacc2 = sb("cacc2", 128, TB, BF16, layer="E1a", barrier=bar0)
        rows3 = sb("rows3", 1, 4 * TB, layer="E1a", barrier=bar0)

        dma("sy", convw_s[:, :], convw_i, w=[R("convw_s")])
        dma("pl", xpw_s[:, :].rearrange("p (cc w) -> p cc w", w=80),
            xpwt.rearrange("(cc p) w -> p cc w", p=128), w=[R("xpw_s")])
        op("dv", lambda e: e.memset(lhsT2[0:2, :], 1.0), w=[R("lhsT2")])
        op("dv", lambda e: e.memset(rhs2[0:2, :], 1.0), w=[R("rhs2")])

        wbufs = [(wstr_a, R("wstr_a")), (wstr_b, R("wstr_b"))]
        xcwbufs = [(xcT_w, R("xcT_w")), (xcT_w2, R("xcT_w2"))]
        accbufs = [(cacc, R("cacc")), (cacc2, R("cacc2"))]
        inpsum = [(pA, R("pA")), (pB, R("pB"))]

        for blk in range(NB):
            xnb_t, xnb_r = xnbufs[blk % 2]
            dma("sy", xnb_t[:, :].rearrange("p (dc t) -> p dc t", t=TB),
                xnv_d[:, :, blk * TB:(blk + 1) * TB],
                r=[R("xn_d")], w=[xnb_r], label=f"p3xn{blk}")
            def ld_ipx(g, blk=blk):
                wb_t, wb_r = wbufs[(blk * 4 + g) % 2]
                dma("pl", wb_t[:, :].rearrange("p (dc h) -> p dc h", h=TB),
                    ipwt_x[:, g * 512:(g + 1) * 512].rearrange(
                        "(dc p) h -> p dc h", p=128),
                    w=[wb_r], label=f"wipx{blk}_{g}")
            ld_ipx(0)
            ld_ipx(1)
            for g in range(4):
                wb_t, wb_r = wbufs[(blk * 4 + g) % 2]
                for cci in range(4):
                    cc = g * 4 + cci
                    pp_t, pp_r = inpsum[cc % 2]
                    for dc in range(DC):
                        mm(pp_t[:, :], wb_t[:, dc * TB + cci * 128:
                                            dc * TB + (cci + 1) * 128],
                           xnb_t[:, dc * TB:(dc + 1) * TB],
                           dc == 0, dc == DC - 1,
                           r=[wb_r, xnb_r], w=[pp_r],
                           label=f"ipx{blk}_{cc}_{dc}")
                    if blk == 0:
                        for dc in range(DC):
                            mm(pD[:, 0:HALO],
                               wb_t[:, dc * TB + cci * 128:
                                    dc * TB + (cci + 1) * 128],
                               xnh[:, dc * HALO:(dc + 1) * HALO],
                               dc == 0, dc == DC - 1,
                               r=[wb_r, R("xnh")], w=[R("pD")],
                               label=f"iph{cc}_{dc}")
                        op("dv", lambda e, cc=cc: e.tensor_copy(
                            xi_blk[:, cc * XW: cc * XW + HALO], pD[:, 0:HALO]),
                            r=[R("pD")], w=[R(f"xi{cc}")], label=f"iphe{cc}")
                    act(xi_blk[:, cc * XW + HALO: cc * XW + XW],
                        pp_t[:, :], AF.Copy, r=[pp_r], w=[R(f"xi{cc}")],
                        label=f"ipxe{blk}_{cc}")
                if g < 2:
                    ld_ipx(g + 2)

            for cc in range(CC):
                ab, ar = accbufs[cc % 2]
                base = cc * XW
                op("dv", lambda e, cc=cc, ab=ab, base=base: e.tensor_scalar(
                    ab[:, :], xi_blk[:, base: base + TB],
                    convw_s[:, cc * 4: cc * 4 + 1],
                    convb_s[:, cc: cc + 1], ALU.mult, ALU.add),
                   r=[R(f"xi{cc}"), R("convw_s"), R("convb_s")], w=[ar],
                   label=f"cv0_{blk}_{cc}")
                for k in range(1, 4):
                    op("dv", lambda e, k=k, ab=ab, base=base, cc=cc:
                       e.scalar_tensor_tensor(
                           ab[:, :], xi_blk[:, base + k: base + k + TB],
                           convw_s[:, cc * 4 + k: cc * 4 + k + 1], ab[:, :],
                           ALU.mult, ALU.add),
                       r=[R(f"xi{cc}"), R("convw_s"), ar], w=[ar],
                       label=f"cv{k}_{blk}_{cc}")
                if blk < NB - 1:
                    op("dv", lambda e, base=base: e.tensor_copy(
                        xi_blk[:, base: base + HALO],
                        xi_blk[:, base + TB: base + TB + HALO]),
                       r=[R(f"xi{cc}")], w=[R(f"xi{cc}")],
                       label=f"tail{blk}_{cc}")
                act(xi_blk[:, base + HALO: base + XW], ab[:, :], AF.Silu,
                    r=[ar], w=[R(f"xi{cc}")], label=f"silu{blk}_{cc}")

            for cc in range(CC):
                mm(pD[0:80, 0:512], xpw_s[:, cc * 80:(cc + 1) * 80],
                   xi_blk[:, cc * XW + HALO: cc * XW + XW],
                   cc == 0, cc == CC - 1,
                   r=[R("xpw_s"), R(f"xi{cc}")], w=[R("pD")],
                   label=f"xp{blk}_{cc}")
            op("dv", lambda e, blk=blk: e.tensor_copy(
                ball[:, blk * TB:(blk + 1) * TB], pD[0:16, 0:512]),
                r=[R("pD")], w=[R("ball")], label=f"ballev{blk}")
            op("dv", lambda e, blk=blk: e.tensor_copy(
                call[:, blk * TB:(blk + 1) * TB], pD[32:48, 0:512]),
                r=[R("pD")], w=[R("call")], label=f"callev{blk}")
            act(rows3[0:1, 2 * TB:3 * TB], pD[64:65, 0:512], AF.Relu,
                r=[R("pD")], w=[R("rows3")], label=f"sp_r{blk}")
            act(rows3[0:1, 3 * TB:4 * TB], pD[64:65, 0:512], AF.Abs,
                r=[R("pD")], w=[R("rows3")], label=f"sp_a{blk}")
            act(rows3[0:1, 3 * TB:4 * TB], rows3[0:1, 3 * TB:4 * TB], AF.Exp,
                scale=-1.0, r=[R("rows3")], w=[R("rows3")], label=f"sp_e{blk}")
            act(rows3[0:1, 3 * TB:4 * TB], rows3[0:1, 3 * TB:4 * TB], AF.Ln,
                bias=cst2[0:1, 1:2], r=[R("rows3"), R("cst2")],
                w=[R("rows3")], label=f"sp_l{blk}")
            op("dv", lambda e, blk=blk, rows3=rows3: e.tensor_add(
                delta_row[0:1, blk * TB:(blk + 1) * TB],
                rows3[0:1, 2 * TB:3 * TB], rows3[0:1, 3 * TB:4 * TB]),
               r=[R("rows3")], w=[R("delta_row")], label=f"sp_d{blk}")

            for qi in range(QPB):
                q = blk * QPB + qi
                xw_t, xw_r = xcwbufs[q % 2]
                for g4 in range(4):
                    bank_t = pTrA if g4 % 2 == 0 else pTrB
                    bank_r = R("pTrA") if g4 % 2 == 0 else R("pTrB")
                    for cci in range(4):
                        cc = g4 * 4 + cci
                        op("pe", lambda e, cc=cc, qi=qi, cci=cci,
                           bank_t=bank_t: e.transpose(
                               bank_t[:, cci * 128:(cci + 1) * 128],
                               xi_blk[:, cc * XW + HALO + qi * 128:
                                      cc * XW + HALO + (qi + 1) * 128],
                               identb[:, :]),
                           r=[R(f"xi{cc}"), R("identb")], w=[bank_r],
                           label=f"xct{q}_{cc}")
                    op("dv", lambda e, g4=g4, xw_t=xw_t, bank_t=bank_t:
                       e.tensor_copy(xw_t[:, g4 * 512:(g4 + 1) * 512],
                                     bank_t[:, :]),
                       r=[bank_r], w=[xw_r], label=f"xcte{q}_{g4}")
                dma("sy", xct_d[q * 128:(q + 1) * 128, :], xw_t[:, :],
                    r=[xw_r], w=[R("xct_d")], label=f"xctst{q}")

        bar1a = retire("xi_blk", "xcT_w", "xcT_w2", "convw_s", "xpw_s",
                       "cacc", "cacc2", "rows3",
                       *[f"xi{cc}" for cc in range(CC)])
        close_layer("E1a")

    bar2 = bar1a
    if P4_ON and P3_ON:
        btilde = sb("btilde", NS, T, layer="E1b", barrier=bar1a)
        khat = sb("khat", NS, T, BF16, layer="E1b", barrier=bar1a)
        efk = sb("efk", NS, T, layer="E1b", barrier=bar1a)
        hw_a = sb("hw_a", NS, DI, layer="E1b", barrier=bar1a)
        hw_b = sb("hw_b", NS, DI, layer="E1b", barrier=bar1a)
        rowsB = sb("rowsB", 1, T, layer="E1b", barrier=bar1a)
        ch16 = sb("ch16", NS, 128, layer="E1b", barrier=bar1a)
        csch = sb("csch", NS, 128, layer="E1b", barrier=bar1a)
        negch = sb("negch", NS, 128, layer="E1b", barrier=bar1a)
        skch = sb("skch", NS, 128, layer="E1b", barrier=bar1a)

        op("dv", lambda e: e.memset(hw_a[:, :], 0.0), w=[R("hw_a")])

        dma("pl", ch16[:, :], delta_row[0:1, :], r=[R("delta_row")],
            w=[R("ch16")], label="dch")
        op("dv", lambda e: e.tensor_tensor_scan(
            csch[:, :], ch16[:, :], ch16[:, :], 0.0, ALU.add, ALU.bypass),
           r=[R("ch16")], w=[R("csch")], label="cumsum")
        op("dv", lambda e: e.tensor_scalar_mul(negch[:, :], csch[:, :], -1.0),
           r=[R("csch")], w=[R("negch")], label="negcs")
        op("dv", lambda e: e.tensor_scalar(
            skch[:, :], csch[:, :], csch[:, 127:128], -1.0,
            ALU.subtract, ALU.mult),
           r=[R("csch")], w=[R("skch")], label="skdiff")
        dma("pl", rhs2[0:1, :], csch[:, :], r=[R("csch")], w=[R("rhs2")],
            label="csrow")
        dma("pl", lhsT2[1:2, :], negch[:, :], r=[R("negch")], w=[R("lhsT2")],
            label="negrow")
        dma("pl", rowsB[0:1, :], skch[:, :], r=[R("skch")], w=[R("rowsB")],
            label="skrow")
        dma("pl", rows16[0:1, 0:16], csch[:, 127:128], r=[R("csch")],
            w=[R("rows16")], label="srow")
        op("dv", lambda e: e.tensor_tensor_scan(
            rows16[0:1, 16:32], rows16[0:1, 0:16], rows16[0:1, 0:16], 0.0,
            ALU.add, ALU.bypass),
           r=[R("rows16")], w=[R("rows16")], label="csS")
        op("dv", lambda e: e.tensor_sub(
            rows16[0:1, 32:48], rows16[0:1, 16:32], rows16[0:1, 0:16]),
           r=[R("rows16")], w=[R("rows16")], label="csx")

        mm(pS[0:NS, 0:NS], ones_f[0:1, 0:NS], rows16[0:1, 0:16], True, True,
           r=[R("ones_f"), R("rows16")], w=[R("pS")], label="sbc16")
        act(Dt16[:, :], pS[0:NS, 0:NS], AF.Exp, scale=a_col[:, 0:1],
            r=[R("pS"), R("a_col")], w=[R("Dt16")], label="dt16")
        mm(pS[0:NS, 0:NS], ones_f[0:1, 0:NS], rows16[0:1, 32:48], True, True,
           r=[R("ones_f"), R("rows16")], w=[R("pS")], label="csbc16")
        act(Pt16[:, :], pS[0:NS, 0:NS], AF.Exp, scale=a_col[:, 0:1],
            r=[R("pS"), R("a_col")], w=[R("Pt16")], label="pt16")

        for g in range(4):
            sl = slice(g * 512, (g + 1) * 512)
            op("dv", lambda e, sl=sl: e.tensor_copy(
                rows_b[0:1, 0:512], delta_row[0:1, sl]),
               r=[R("delta_row")], w=[R("rows_b")], label=f"deltab{g}")
            mm(pBc[0:NS, 0:512], ones_b[0:1, 0:NS], rows_b[0:1, 0:512],
               True, True, r=[R("ones_b"), R("rows_b")], w=[R("pBc")],
               label=f"dbcast{g}")
            op("dv", lambda e, sl=sl: e.tensor_mul(
                btilde[:, sl], ball[:, sl], pBc[0:NS, 0:512]),
               r=[R("ball"), R("pBc")], w=[R("btilde")], label=f"btl{g}")
            mm(pBc2[0:NS, 0:512], ones_f[0:1, 0:NS], rowsB[0:1, sl],
               True, True, r=[R("ones_f"), R("rowsB")], w=[R("pBc2")],
               label=f"skbc{g}")
            act(efk[:, sl], pBc2[0:NS, 0:512], AF.Exp, scale=a_col[:, 0:1],
                r=[R("pBc2"), R("a_col")], w=[R("efk")], label=f"efk{g}")
            op("dv", lambda e, sl=sl: e.tensor_mul(
                khat[:, sl], btilde[:, sl], efk[:, sl]),
               r=[R("btilde"), R("efk")], w=[R("khat")], label=f"kh{g}")
            mm(pBc[0:NS, 0:512], ones_f[0:1, 0:NS], rhs2[0:1, sl], True, True,
               r=[R("ones_f"), R("rhs2")], w=[R("pBc")], label=f"csbc{g}")
            act(efk[:, sl], pBc[0:NS, 0:512], AF.Exp, scale=a_col[:, 0:1],
                r=[R("pBc"), R("a_col")], w=[R("efk")], label=f"efq{g}")
            op("dv", lambda e, sl=sl: e.tensor_mul(
                qt_all[:, sl], call[:, sl], efk[:, sl]),
               r=[R("call"), R("efk")], w=[R("qt_all")], label=f"qt{g}")

        for q in range(NQ):
            op("pe", lambda e, q=q: e.transpose(
                pD[:, 0:NS], btilde[:, q * 128:(q + 1) * 128],
                identf[0:NS, 0:NS]),
               r=[R("btilde"), R("identf")], w=[R("pD")], label=f"btT{q}")
            op("dv", lambda e, q=q: e.tensor_copy(
                btT_all[:, q * NS:(q + 1) * NS], pD[:, 0:NS]),
               r=[R("pD")], w=[R("btT_all")], label=f"btTe{q}")
            op("pe", lambda e, q=q: e.transpose(
                pTrA[:, 0:NS], khat[:, q * 128:(q + 1) * 128],
                identb[0:NS, 0:NS]),
               r=[R("khat"), R("identb")], w=[R("pTrA")], label=f"khT{q}")
            op("dv", lambda e, q=q: e.tensor_copy(
                khT_all[:, q * NS:(q + 1) * NS], pTrA[:, 0:NS]),
               r=[R("pTrA")], w=[R("khT_all")], label=f"khTe{q}")

        xcsbufs = [(xcT_s, R("xcT_s")), (xcT_s2, R("xcT_s2"))]
        hwbufs = [(hw_a, R("hw_a")), (hw_b, R("hw_b"))]
        hpsum = [(pA, R("pA")), (pB, R("pB"))]
        for q in range(NQ):
            xs_t, xs_r = xcsbufs[q % 2]
            dma("sy", xs_t[:, :], xct_d[q * 128:(q + 1) * 128, :],
                r=[R("xct_d")], w=[xs_r], label=f"xcld{q}")
            hw_cur, hwc_r = hwbufs[q % 2]
            hw_nxt, hwn_r = hwbufs[(q + 1) % 2]
            dma("sy", hst_d[q * NS:(q + 1) * NS, :], hw_cur[:, :],
                r=[hwc_r], w=[R("hst_d")], label=f"hstst{q}")
            for g in range(4):
                pp_t, pp_r = hpsum[g % 2]
                mm(pp_t[0:NS, 0:512], khT_all[:, q * NS:(q + 1) * NS],
                   xs_t[:, g * 512:(g + 1) * 512], True, True,
                   r=[R("khT_all"), xs_r], w=[pp_r], label=f"hc{q}_{g}")
                sl = slice(g * 512, (g + 1) * 512)
                op("dv", lambda e, sl=sl, q=q, hw_cur=hw_cur, hw_nxt=hw_nxt:
                   e.tensor_scalar(hw_nxt[:, sl], hw_cur[:, sl],
                                   Dt16[:, q:q + 1], None, ALU.mult),
                   r=[hwc_r, R("Dt16")], w=[hwn_r], label=f"hs{q}_{g}")
                op("dv", lambda e, sl=sl, hw_nxt=hw_nxt, pp_t=pp_t:
                   e.tensor_add(hw_nxt[:, sl], hw_nxt[:, sl],
                                pp_t[0:NS, 0:512]),
                   r=[pp_r, hwn_r], w=[hwn_r], label=f"ha{q}_{g}")
        hend_t, hend_r = hwbufs[NQ % 2]
        dma("sy", ccin[0:NS, :], hend_t[:, :], r=[hend_r], w=[R("ccin")],
            label="ccin_h")
        dma("sy", ccin[NS:NS + 1, :], csch[:, :], r=[R("csch")],
            w=[R("ccin")], label="ccin_fill")
        dma("sy", ccin[NS:NS + 1, 0:1], rows16[0:1, 31:32], r=[R("rows16")],
            w=[R("ccin")], label="ccin_s")
        op("pl", lambda e: e.collective_compute(
            "AllGather", ALU.bypass, replica_groups=[list(range(NCORE))],
            ins=[ccin.opt()], outs=[ccout.opt()]),
           r=[R("ccin")], w=[R("ccout")], kind="cc", label="allgather")

        barT = retire("pTrA", "pTrB")
        close_layer("psumT")
        bar1b = retire("btilde", "khat", "efk", "hw_a", "hw_b", "rowsB",
                       "ch16", "csch", "negch", "skch")
        close_layer("E1b")
        bar1c = retire("delta_row", "ball")
        close_layer("E1c")
        bar2 = bar1b + bar1c

    bar3 = bar2
    if P4_ON and P3_ON:
        pC = pst("pC", 128, 512, layer="psumC", barrier=barT)
    else:
        pC = pst("pC", 128, 512, layer="psumC")
    if P5_ON and P4_ON and P3_ON:
        szg = sb("szg", 128, CC * TB, BF16, layer="E2", barrier=bar2)
        hst_s = sb("hst_s", NS, DI, layer="E2", barrier=bar2)
        htr_b = sb("htr_b", NS, DI, BF16, layer="E2", barrier=bar2)
        hag = sb("hag", NS, DI, layer="E2", barrier=bar2)
        h_in = sb("h_in", NS, DI, layer="E2", barrier=bar2)
        xcdT = sb("xcdT", 128, DI, BF16, layer="E2", barrier=bar2)
        crow = sb("crow", 1, DI, BF16, layer="E2", barrier=bar2)
        crow2 = sb("crow2", 1, DI, BF16, layer="E2", barrier=bar2)
        dbcT = sb("dbcT", 128, DI, BF16, layer="E2", barrier=bar2)

        dma("pl", crow[0:1, :], drow_i, w=[R("crow")], label="drow")
        for g in range(4):
            mm(pBc[:, :], ones_b[0:1, :], crow[0:1, g * 512:(g + 1) * 512],
               True, True, r=[R("ones_b"), R("crow")], w=[R("pBc")],
               label=f"dbc{g}")
            op("dv", lambda e, g=g: e.tensor_copy(
                dbcT[:, g * 512:(g + 1) * 512], pBc[:, :]),
                r=[R("pBc")], w=[R("dbcT")], label=f"dbcev{g}")

        wbufs = [(wstr_a, R("wstr_a")), (wstr_b, R("wstr_b"))]
        inpsum = [(pA, R("pA")), (pB, R("pB"))]
        op("dv", lambda e: e.memset(h_in[:, :], 0.0), w=[R("h_in")])

        zdone = set()

        def emit_zblock(blk):
            xnb_t, xnb_r = xnbufs[blk % 2]
            dma("sy", xnb_t[:, :].rearrange("p (dc t) -> p dc t", t=TB),
                xnv_d[:, :, blk * TB:(blk + 1) * TB],
                r=[R("xn_d")], w=[xnb_r], label=f"zxn{blk}")
            def ld_ipz(g, blk=blk):
                wb_t, wb_r = wbufs[(blk * 4 + g) % 2]
                dma("pl", wb_t[:, :].rearrange("p (dc h) -> p dc h", h=TB),
                    ipwt_z[:, g * 512:(g + 1) * 512].rearrange(
                        "(dc p) h -> p dc h", p=128),
                    w=[wb_r], label=f"wipz{blk}_{g}")
            ld_ipz(0)
            ld_ipz(1)
            for g in range(4):
                wb_t, wb_r = wbufs[(blk * 4 + g) % 2]
                for cci in range(4):
                    cc = g * 4 + cci
                    pp_t, pp_r = inpsum[cc % 2]
                    for dc in range(DC):
                        mm(pp_t[:, :],
                           wb_t[:, dc * TB + cci * 128:
                                dc * TB + (cci + 1) * 128],
                           xnb_t[:, dc * TB:(dc + 1) * TB],
                           dc == 0, dc == DC - 1,
                           r=[wb_r, xnb_r], w=[pp_r],
                           label=f"ipz{blk}_{cc}_{dc}")
                    act(szg[:, cc * TB:(cc + 1) * TB], pp_t[:, :], AF.Silu,
                        r=[pp_r], w=[R("szg")], label=f"zs{blk}_{cc}")
                if g < 2:
                    ld_ipz(g + 2)
            zdone.add(blk)

        emit_zblock(0)

        sg_src = ccout.rearrange("(c r) w -> c (r w)", r=NS + 1)[
            :, NS * DI: NS * DI + 1]
        op("sy", lambda e: e.dma_start(out=srow8[0:1, :], in_=sg_src,
                                       allow_slow_non_contiguous=True),
           r=[R("ccout")], w=[R("srow8")], kind="dma", label="sgather")
        dma("pl", scol8[:, :], srow8[0:1, :], r=[R("srow8")], w=[R("scol8")],
            label="scol8")
        mm(pS[0:NE, 0:1], gmaskt[:, :], scol8[:, :], True, True,
           r=[R("gmaskt"), R("scol8")], w=[R("pS")], label="gmm")
        op("dv", lambda e: e.tensor_copy(g8[:, :], pS[0:NE, 0:1]),
           r=[R("pS")], w=[R("g8")], label="g8ev")
        dma("pl", rows16[0:1, 48:56], g8[:, :], r=[R("g8")], w=[R("rows16")],
            label="grow")
        mm(pS[0:NS, 0:NE], ones_f[0:1, 0:NS], rows16[0:1, 48:56], True, True,
           r=[R("ones_f"), R("rows16")], w=[R("pS")], label="gbc")
        act(coef16[:, :], pS[0:NS, 0:NE], AF.Exp, scale=a_col[:, 0:1],
            r=[R("pS"), R("a_col")], w=[R("coef16")], label="coefexp")
        op("dv", lambda e: e.tensor_mul(coef16[:, :], coef16[:, :],
                                        selbc[:, :]),
           r=[R("coef16"), R("selbc")], w=[R("coef16")], label="coefsel")
        for j in range(NCORE):
            dma("sy", hag[:, :], ccout[j * (NS + 1): j * (NS + 1) + NS, :],
                r=[R("ccout")], w=[R("hag")], label=f"hagld{j}")
            op("dv", lambda e, j=j: e.scalar_tensor_tensor(
                h_in[:, :], hag[:, :], coef16[:, j:j + 1], h_in[:, :],
                ALU.mult, ALU.add),
               r=[R("hag"), R("coef16"), R("h_in")], w=[R("h_in")],
               label=f"hin{j}")

        xcsbufs = [(xcT_s, R("xcT_s")), (xcT_s2, R("xcT_s2"))]
        enbufs = [(en_a, R("en_a")), (en_b, R("en_b"))]
        ypsum = [(pC, R("pC")), (pD, R("pD"))]
        cbcpsum = [(pBc, R("pBc")), (pBc2, R("pBc2"))]
        for blk in range(NB):
            if blk not in zdone:
                emit_zblock(blk)
            for qi in range(QPB):
                q = blk * QPB + qi
                xs_t, xs_r = xcsbufs[q % 2]
                dma("sy", xs_t[:, :], xct_d[q * 128:(q + 1) * 128, :],
                    r=[R("xct_d")], w=[xs_r], label=f"s2xc{q}")
                dma("sy", hst_s[:, :], hst_d[q * NS:(q + 1) * NS, :],
                    r=[R("hst_d")], w=[R("hst_s")], label=f"s2hs{q}")
                mm(pS[:, 0:128], lhsT2[:, q * 128:(q + 1) * 128],
                   rhs2[:, q * 128:(q + 1) * 128], True, True,
                   r=[R("lhsT2"), R("rhs2")], w=[R("pS")], label=f"ds{q}")
                dsm_t, dsm_r = [(dsm, R("dsm")), (dsm2, R("dsm2"))][q % 2]
                mtf_t, mtf_r = [(mt_f, R("mt_f")),
                                (mt_f2, R("mt_f2"))][q % 2]
                mtt_t, mtt_r = [(mt_t, R("mt_t")),
                                (mt_t2, R("mt_t2"))][q % 2]
                mtb_t, mtb_r = [(mtb, R("mtb")), (mtb2, R("mtb2"))][q % 2]
                enbufs_q = [[(en_a, R("en_a")), (en_b, R("en_b"))],
                            [(en_c, R("en_c")), (en_d, R("en_d"))]][q % 2]
                op("dv", lambda e, dsm_t=dsm_t: e.tensor_scalar_max(
                    dsm_t[:, :], pS[:, 0:128], 0.0),
                   r=[R("pS")], w=[dsm_r], label=f"dsm{q}")
                cr_t, cr_r = [(crow, R("crow")), (crow2, R("crow2"))][q % 2]
                dma("pl", cr_t[0:1, :], call[:, q * 128:(q + 1) * 128],
                    r=[R("call")], w=[cr_r], label=f"crow{q}")
                op("dv", lambda e, q=q: e.scalar_tensor_tensor(
                    htr_b[:, :], h_in[:, :], Pt16[:, q:q + 1], hst_s[:, :],
                    ALU.mult, ALU.add),
                   r=[R("h_in"), R("Pt16"), R("hst_s")], w=[R("htr_b")],
                   label=f"htr{q}")
                op("dv", lambda e, xs_t=xs_t: e.tensor_mul(
                    xcdT[:, :], xs_t[:, :], dbcT[:, :]),
                   r=[xs_r, R("dbcT")], w=[R("xcdT")], label=f"xcd{q}")
                for n in range(NS):
                    en_t, en_r = enbufs_q[n % 2]
                    act(en_t[:, :], dsm_t[:, :], AF.Exp,
                        scale=abc_s[:, n:n + 1],
                        r=[dsm_r, R("abc_s")], w=[en_r], label=f"en{q}_{n}")
                    cb_t, cb_r = cbcpsum[(n // 4) % 2]
                    if n % 4 == 0:
                        g2 = n // 4
                        mm(cb_t[:, :], ones_b[0:1, :],
                           cr_t[0:1, g2 * 512:(g2 + 1) * 512], True, True,
                           r=[R("ones_b"), cr_r], w=[cb_r],
                           label=f"cbc{q}_{g2}")
                    csl = slice((n % 4) * 128, (n % 4 + 1) * 128)
                    if n == 0:
                        op("dv", lambda e, en_t=en_t, q=q, cb_t=cb_t, csl=csl,
                           mtf_t=mtf_t: e.scalar_tensor_tensor(
                               mtf_t[:, :], en_t[:, :],
                               btT_all[:, q * NS: q * NS + 1],
                               cb_t[:, csl], ALU.mult, ALU.mult),
                           r=[en_r, R("btT_all"), cb_r], w=[mtf_r],
                           label=f"mt{q}_{n}")
                    else:
                        op("dv", lambda e, en_t=en_t, q=q, n=n, cb_t=cb_t,
                           csl=csl, mtt_t=mtt_t: e.scalar_tensor_tensor(
                               mtt_t[:, :], en_t[:, :],
                               btT_all[:, q * NS + n: q * NS + n + 1],
                               cb_t[:, csl], ALU.mult, ALU.mult),
                           r=[en_r, R("btT_all"), cb_r], w=[mtt_r],
                           label=f"mtt{q}_{n}")
                        op("dv", lambda e, mtf_t=mtf_t, mtt_t=mtt_t:
                           e.tensor_add(
                               mtf_t[:, :], mtf_t[:, :], mtt_t[:, :]),
                           r=[mtt_r, mtf_r], w=[mtf_r],
                           label=f"mta{q}_{n}")
                op("dv", lambda e, mtb_t=mtb_t, mtf_t=mtf_t: e.tensor_mul(
                    mtb_t[:, :], mtf_t[:, :], masktri[:, :]),
                   r=[mtf_r, R("masktri")], w=[mtb_r], label=f"mtb{q}")
                for ccg in range(4):
                    yp_t, yp_r = ypsum[(q * 4 + ccg) % 2]
                    for cci in range(4):
                        cc = ccg * 4 + cci
                        osl = slice(cci * 128, (cci + 1) * 128)
                        mm(yp_t[:, osl], htr_b[:, cc * 128:(cc + 1) * 128],
                           qt_all[:, q * 128:(q + 1) * 128], True, False,
                           r=[R("htr_b"), R("qt_all")], w=[yp_r],
                           label=f"y1_{q}_{cc}")
                        mm(yp_t[:, osl], xs_t[:, cc * 128:(cc + 1) * 128],
                           mtb_t[:, :], False, False,
                           r=[xs_r, mtb_r], w=[yp_r], label=f"y2_{q}_{cc}")
                        mm(yp_t[:, osl], xcdT[:, cc * 128:(cc + 1) * 128],
                           identb[:, :], False, True,
                           r=[R("xcdT"), R("identb")], w=[yp_r],
                           label=f"y3_{q}_{cc}")
                    for cci in range(4):
                        cc = ccg * 4 + cci
                        osl = slice(cci * 128, (cci + 1) * 128)
                        gsl = slice(cc * TB + qi * 128,
                                    cc * TB + (qi + 1) * 128)
                        op("dv", lambda e, yp_t=yp_t, osl=osl, gsl=gsl:
                           e.tensor_mul(szg[:, gsl], yp_t[:, osl],
                                        szg[:, gsl]),
                           r=[yp_r, R("szg")], w=[R("szg")],
                           label=f"gate{q}_{cc}")
            def ld_op(g, blk=blk):
                wb_t, wb_r = wbufs[(blk * 4 + g) % 2]
                dma("pl", wb_t[:, 0:CC * 256].rearrange(
                    "p (cc h) -> p cc h", h=256),
                    opwt[:, g * 256:(g + 1) * 256].rearrange(
                        "(cc p) h -> p cc h", p=128),
                    w=[wb_r], label=f"wop{blk}_{g}")
            ld_op(0)
            ld_op(1)
            for g in range(4):
                wb_t, wb_r = wbufs[(blk * 4 + g) % 2]
                for doi in range(2):
                    do = g * 2 + doi
                    pp_t, pp_r = inpsum[do % 2]
                    for cc in range(CC):
                        mm(pp_t[:, :],
                           wb_t[:, cc * 256 + doi * 128:
                                cc * 256 + (doi + 1) * 128],
                           szg[:, cc * TB:(cc + 1) * TB],
                           cc == 0, cc == CC - 1,
                           r=[wb_r, R("szg")], w=[pp_r],
                           label=f"op{blk}_{do}_{cc}")
                    op("dv", lambda e, do=do, blk=blk, pp_t=pp_t:
                       e.tensor_add(
                           y[:, do * T + blk * TB: do * T + (blk + 1) * TB],
                           y[:, do * T + blk * TB: do * T + (blk + 1) * TB],
                           pp_t[:, :]),
                       r=[pp_r, R("y")], w=[R("y")], label=f"yop{blk}_{do}")
                if g < 2:
                    ld_op(g + 2)

        bar3 = retire("szg", "hst_s", "htr_b", "hag", "h_in",
                      "xcdT", "crow", "dbcT")
        close_layer("E2")
        bar3 = bar3 + retire("call", "lhsT2", "rhs2", "qt_all", "wstr_a",
                             "wstr_b", "xcT_s", "xcT_s2", "btT_all",
                             "khT_all", "dsm", "dsm2", "en_a", "en_b",
                             "en_c", "en_d", "mt_f", "mt_f2", "mt_t",
                             "mt_t2", "mtb", "mtb2")
        close_layer("E1s")

    if P6_ON:
        w12A = sb("w12A", 128, DC * DI, BF16, layer="E3", barrier=bar3)
        w12B = sb("w12B", 128, DC * DI, BF16, layer="E3", barrier=bar3)
        w3_s = sb("w3_s", 128, HK * D, BF16, layer="E3", barrier=bar3)
        hid_blk = sb("hid_blk", 128, HK * TB, BF16, layer="E3", barrier=bar3)
        s1_a = sb("s1_a", 128, TB, layer="E3", barrier=bar3)
        s1_b = sb("s1_b", 128, TB, layer="E3", barrier=bar3)
        t2_a = sb("t2_a", 128, TB, layer="E3", barrier=bar3)
        t2_b = sb("t2_b", 128, TB, layer="E3", barrier=bar3)
        pbc_s = sb("pbc_s", 128, TB, layer="E3", barrier=bar3)

        mm1psum = [[(pA, R("pA")), (pC, R("pC"))],
                   [(pB, R("pB")), (pD, R("pD"))]]
        mm2psum = [(pS, R("pS")), (pBc, R("pBc"))]
        s1bufs = [(s1_a, R("s1_a")), (s1_b, R("s1_b"))]
        t2bufs = [(t2_a, R("t2_a")), (t2_b, R("t2_b"))]

        def load_expert_w(e_i, part):
            if part in ("A", "B"):
                tgt = w12A if part == "A" else w12B
                res = R("w12A") if part == "A" else R("w12B")
                off = 0 if part == "A" else 1024
                dma("pl", tgt[:, :].rearrange(
                    "p (dc h) -> p dc h", h=DI)[:, :, 0:1024],
                    w12t[e_i][:, off:off + 1024].rearrange(
                        "(dc p) h -> p dc h", p=128),
                    w=[res], label=f"w{part}1_{e_i}")
                dma("pl", tgt[:, :].rearrange(
                    "p (dc h) -> p dc h", h=DI)[:, :, 1024:2048],
                    w12t[e_i][:, DI + off:DI + off + 1024].rearrange(
                        "(dc p) h -> p dc h", p=128),
                    w=[res], label=f"w{part}2_{e_i}")
            else:
                dma("pl", w3_s[:, :].rearrange("p (hc h) -> p hc h", h=D),
                    w3t[e_i].rearrange("(hc p) h -> p hc h", p=128),
                    w=[R("w3_s")], label=f"w3_{e_i}")

        load_expert_w(0, "A")
        load_expert_w(0, "B")
        load_expert_w(0, "3")
        for e_i in range(NE):
            for blk in range(NB):
                xnb_t, xnb_r = xnbufs[(e_i * NB + blk) % 2]
                dma("sy", xnb_t[:, :].rearrange("p (dc t) -> p dc t", t=TB),
                    xnv_d[:, :, blk * TB:(blk + 1) * TB],
                    r=[R("xn_d")], w=[xnb_r], label=f"fxn{e_i}_{blk}")
                dma("pl", rows_b[0:1, 0:512],
                    prow_d[e_i:e_i + 1, blk * TB:(blk + 1) * TB],
                    r=[R("prow_d")], w=[R("rows_b")], label=f"fpr{e_i}_{blk}")
                mm(pBc2[:, :], ones_b[0:1, :], rows_b[0:1, 0:512], True, True,
                   r=[R("ones_b"), R("rows_b")], w=[R("pBc2")],
                   label=f"fpb{e_i}_{blk}")
                act(pbc_s[:, :], pBc2[:, :], AF.Copy,
                    r=[R("pBc2")], w=[R("pbc_s")], label=f"fpe{e_i}_{blk}")
                for k in range(HK):
                    wsrc = w12A if k < 8 else w12B
                    wres = R("w12A") if k < 8 else R("w12B")
                    kk = k % 8
                    p1_t, p1_r = mm1psum[0][k % 2]
                    p2_t, p2_r = mm1psum[1][k % 2]
                    for dc in range(DC):
                        mm(p1_t[:, :],
                           wsrc[:, dc * DI + kk * 128:
                                dc * DI + (kk + 1) * 128],
                           xnb_t[:, dc * TB:(dc + 1) * TB],
                           dc == 0, dc == DC - 1,
                           r=[wres, xnb_r], w=[p1_r],
                           label=f"f1_{e_i}_{blk}_{k}_{dc}")
                    for dc in range(DC):
                        mm(p2_t[:, :],
                           wsrc[:, dc * DI + 1024 + kk * 128:
                                dc * DI + 1024 + (kk + 1) * 128],
                           xnb_t[:, dc * TB:(dc + 1) * TB],
                           dc == 0, dc == DC - 1,
                           r=[wres, xnb_r], w=[p2_r],
                           label=f"f2_{e_i}_{blk}_{k}_{dc}")
                    s1_t, s1_r = s1bufs[k % 2]
                    t2_t, t2_r = t2bufs[k % 2]
                    act(s1_t[:, :], p1_t[:, :], AF.Silu,
                        r=[p1_r], w=[s1_r], label=f"fs{e_i}_{blk}_{k}")
                    op("dv", lambda e, p2_t=p2_t, t2_t=t2_t: e.tensor_mul(
                        t2_t[:, :], p2_t[:, :], pbc_s[:, :]),
                       r=[p2_r, R("pbc_s")], w=[t2_r],
                       label=f"ft{e_i}_{blk}_{k}")
                    heng = "pl" if k % 2 == 0 else "dv"
                    op(heng, lambda e, s1_t=s1_t, t2_t=t2_t, k=k:
                       e.tensor_mul(hid_blk[:, k * TB:(k + 1) * TB],
                                    s1_t[:, :], t2_t[:, :]),
                       r=[s1_r, t2_r], w=[R("hid_blk")],
                       label=f"fh{e_i}_{blk}_{k}")
                for do in range(DC):
                    po_t, po_r = mm2psum[do % 2]
                    for k in range(HK):
                        mm(po_t[:, :],
                           w3_s[:, k * D + do * 128: k * D + (do + 1) * 128],
                           hid_blk[:, k * TB:(k + 1) * TB],
                           k == 0, k == HK - 1,
                           r=[R("w3_s"), R("hid_blk")], w=[po_r],
                           label=f"f3_{e_i}_{blk}_{do}_{k}")
                    op("dv", lambda e, do=do, blk=blk, po_t=po_t:
                       e.tensor_add(
                           y[:, do * T + blk * TB: do * T + (blk + 1) * TB],
                           y[:, do * T + blk * TB: do * T + (blk + 1) * TB],
                           po_t[:, :]),
                       r=[po_r, R("y")], w=[R("y")],
                       label=f"fy{e_i}_{blk}_{do}")
                if blk == NB - 1 and e_i + 1 < NE:
                    load_expert_w(e_i + 1, "A")
                    load_expert_w(e_i + 1, "B")
                    load_expert_w(e_i + 1, "3")

    tst = dma("sy", outt.rearrange("(dc p) t -> p dc t", p=128),
              y[:, :].rearrange("p (dc t) -> p dc t", t=T),
              r=[R("y")], w=[R("yfin")], label="ystore")
    b.do("sy", None, after=[tst], kind="nop", label="fence")

    b.finalize_and_emit(nc)
    return nc


_NC_CACHE = {}


def kernel(**inputs):
    x = np.asarray(inputs["L0_attn_x"], np.float32)[0]
    ipw = np.asarray(inputs["L0_attn_in_proj_w"], np.float32)
    convw = np.asarray(inputs["L0_attn_conv_w"], np.float32)[:, 0, :]
    convb = np.asarray(inputs["L0_attn_conv_b"], np.float32)
    xpw = np.asarray(inputs["L0_attn_x_proj_w"], np.float32)
    A_log = np.asarray(inputs["L0_attn_A_log"], np.float32)
    Dsk = np.asarray(inputs["L0_attn_D"], np.float32)
    opw = np.asarray(inputs["L0_attn_out_proj_w"], np.float32)
    w12 = np.asarray(inputs["L0_ffn_w12"], np.float32)
    w3 = np.asarray(inputs["L0_ffn_w3"], np.float32)
    gw = np.asarray(inputs["L0_ffn_gate_w"], np.float32)

    xT = np.ascontiguousarray(x.T)
    ipwt_x = np.ascontiguousarray(ipw[:DI].T)
    ipwt_z = np.ascontiguousarray(ipw[DI:].T)
    xpwt = np.zeros((DI, 80), np.float32)
    xpwt[:, 0:16] = xpw[1:17].T
    xpwt[:, 32:48] = xpw[17:33].T
    xpwt[:, 64] = xpw[0]
    opwt = np.ascontiguousarray(opw.T)
    w12t = np.ascontiguousarray(np.transpose(w12, (0, 2, 1)))
    w3t = np.ascontiguousarray(np.transpose(w3, (0, 2, 1)))
    gwt = np.ascontiguousarray(gw.T)
    convw_p = np.ascontiguousarray(
        convw.reshape(CC, 128, 4).transpose(1, 0, 2).reshape(128, CC * 4))
    convb_p = np.ascontiguousarray(convb.reshape(CC, 128).T)
    drow = np.ascontiguousarray(Dsk.reshape(1, DI))
    alog = np.ascontiguousarray(A_log.reshape(1, NS))
    ones2d = np.ones((128, 128), np.float32)
    identf = np.eye(128, dtype=np.float32)
    idx = np.arange(128)
    masktri = (idx[:, None] <= idx[None, :]).astype(np.float32)

    in_maps = []
    for c in range(NCORE):
        lo = c * T
        if c == 0:
            xts = np.concatenate(
                [np.zeros((D, HALO), np.float32), xT[:, 0:T]], axis=1)
        else:
            xts = xT[:, lo - HALO: lo + T]
        selbc = np.tile((np.arange(NE) < c).astype(np.float32)[None, :],
                        (NS, 1))
        gmaskt = np.zeros((NE, NE), np.float32)
        for j in range(NE):
            for k in range(NE):
                if j + 1 <= k <= c - 1:
                    gmaskt[k, j] = 1.0
        in_maps.append(dict(
            xt=np.ascontiguousarray(xts), ipwt_x=ipwt_x, ipwt_z=ipwt_z,
            xpwt=xpwt, opwt=opwt, w12t=w12t, w3t=w3t, gwt=gwt,
            convw_p=convw_p, convb_p=convb_p, drow=drow, alog=alog,
            ones2d=ones2d, identf_h=identf, masktri_h=masktri,
            selbc_h=np.ascontiguousarray(selbc), gmaskt_h=gmaskt))

    if "nc" not in _NC_CACHE:
        _NC_CACHE["nc"] = build_nc()
    nc = _NC_CACHE["nc"]
    res = run_bass_kernel_spmd(nc, in_maps, core_ids=list(range(NCORE)))
    if DEBUG:
        kernel.dbg = res.results
    outs = [np.asarray(r["outt"], np.float32).T for r in res.results]
    return np.concatenate(outs, axis=0)[None]




# version 09
# speedup vs baseline: 1.0680x; 1.0680x over previous
import contextlib
import numpy as np
import concourse.bass as bass
import concourse.mybir as mybir
from concourse.bass_utils import run_bass_kernel_spmd

F32 = mybir.dt.float32
BF16 = mybir.dt.bfloat16
AF = mybir.ActivationFunctionType
ALU = mybir.AluOpType
AX = mybir.AxisListType

NCORE = 8
L = 16384
D = 1024
DI = 2048
NS = 16
T = L // NCORE
TB = 512
NB = T // TB
Q = 128
NQ = T // Q
QPB = TB // Q
DC = D // 128
CC = DI // 128
NE = 8
HK = 16
HALO = 3
XW = HALO + TB
EPS = 1e-5
BIG = 1e30

P3_ON = True
P4_ON = True
P5_ON = True
P6_ON = True
DEBUG = False


class _Op:
    __slots__ = ("eng", "fn", "after", "kind", "inc", "val", "label", "lane")

    def __init__(self, eng, fn, after, kind, label):
        self.eng, self.fn, self.after, self.kind = eng, fn, after, kind
        self.inc = False
        self.val = None
        self.label = label
        self.lane = 0


class Builder:
    ENGINES = ("pe", "dv", "ac", "pl", "sy")
    NLANES = 8

    def __init__(self):
        self.streams = {e: [] for e in self.ENGINES}
        self.dmacnt = {e: 0 for e in self.ENGINES}

    def do(self, eng, fn, after=(), kind="compute", label=""):
        seen, flt = set(), []
        for a in after:
            if a is None or id(a) in seen:
                continue
            seen.add(id(a))
            flt.append(a)
        op = _Op(eng, fn, flt, kind, label)
        if kind == "dma":
            op.lane = self.dmacnt[eng] % self.NLANES
            self.dmacnt[eng] += 1
            op.inc = True
        for a in flt:
            if a.eng == "pe" and eng == "pe" and a.kind == "compute" \
                    and kind == "compute":
                continue
            a.inc = True
        self.streams[eng].append(op)
        return op

    def finalize_and_emit(self, nc):
        self.sems = {}
        for e in self.ENGINES:
            self.sems[("c", e)] = nc.semaphore(f"sc_{e}").__enter__()
            self.sems[("cc", e)] = nc.semaphore(f"scc_{e}").__enter__()
            for ln in range(self.NLANES):
                self.sems[("d", e, ln)] = nc.semaphore(
                    f"sd_{e}_{ln}").__enter__()
        cnt = {k: 0 for k in self.sems}
        for e in self.ENGINES:
            for op_ in self.streams[e]:
                if op_.inc:
                    k = self._semkey(op_)
                    cnt[k] += self._amt(op_)
                    op_.val = cnt[k]
        block = nc.Block().__enter__()
        regs = {"pe": block.tensor, "dv": block.vector, "ac": block.scalar,
                "pl": block.gpsimd, "sy": block.sync}
        for ename in self.ENGINES:
            stream = self.streams[ename]
            if not stream:
                continue

            def emit_stream(eng, stream=stream, ename=ename):
                clock = {}
                for op_ in stream:
                    for a in op_.after:
                        if (a.eng == "pe" and ename == "pe"
                                and a.kind == "compute"
                                and op_.kind == "compute"):
                            continue
                        k = self._semkey(a)
                        if clock.get(k, 0) < a.val:
                            eng.wait_ge(self.sems[k], a.val)
                            clock[k] = a.val
                    if op_.kind == "dma" and op_.val > 16:
                        k = self._semkey(op_)
                        if clock.get(k, 0) < op_.val - 16:
                            eng.wait_ge(self.sems[k], op_.val - 16)
                            clock[k] = op_.val - 16
                    inst = op_.fn(eng) if op_.fn is not None else None
                    if op_.inc:
                        inst.then_inc(self.sems[self._semkey(op_)],
                                      self._amt(op_))

            regs[ename](emit_stream)
        block.__exit__(None, None, None)

    @staticmethod
    def _semkey(op_):
        if op_.kind == "dma":
            return ("d", op_.eng, op_.lane)
        if op_.kind == "cc":
            return ("cc", op_.eng)
        return ("c", op_.eng)

    @staticmethod
    def _amt(op_):
        return 16 if op_.kind == "dma" else 1


class Res:
    __slots__ = ("w", "r")

    def __init__(self, barrier=()):
        self.w = None
        self.r = list(barrier)


def build_nc():
    nc = bass.Bass("TRN2", target_bir_lowering=False)
    b = Builder()
    RES = {}

    def op(eng, fn, r=(), w=(), extra=(), kind="compute", label=""):
        deps = list(extra)
        for x in r:
            deps.append(x.w)
        for x in w:
            deps.append(x.w)
            deps.extend(x.r)
        t = b.do(eng, fn, after=deps, kind=kind, label=label)
        for x in r:
            x.r.append(t)
        for x in w:
            x.w = t
            x.r = []
        return t

    def R(name):
        return RES[name]

    def din(name, shape, dtype=F32):
        return nc.dram_tensor(name, list(shape), dtype,
                              kind="ExternalInput").ap()

    xt = din("xt", (D, HALO + T))
    ipwt_x = din("ipwt_x", (D, DI))
    ipwt_z = din("ipwt_z", (D, DI))
    xpwt = din("xpwt", (DI, 80))
    opwt = din("opwt", (DI, D))
    w12t = din("w12t", (NE, D, 2 * DI))
    w3t = din("w3t", (NE, DI, D))
    gwt = din("gwt", (D, NE))
    convw_i = din("convw_p", (128, CC * 4))
    convb_i = din("convb_p", (128, CC))
    drow_i = din("drow", (1, DI))
    alog_i = din("alog", (1, NS))
    ones2d_i = din("ones2d", (128, 128))
    identf_i = din("identf_h", (128, 128))
    masktri_i = din("masktri_h", (128, 128))
    selbc_i = din("selbc_h", (NS, NE))
    gmaskt_i = din("gmaskt_h", (NE, NE))

    outt = nc.dram_tensor("outt", [D, T], F32, kind="ExternalOutput").ap()

    dk = dict(kind="ExternalOutput") if DEBUG else {}
    xn_d = nc.dram_tensor("xn_d", [128, DC * T], BF16, **dk).ap()
    xct_d = nc.dram_tensor("xct_d", [NQ * 128, DI], BF16, **dk).ap()
    hst_d = nc.dram_tensor("hst_d", [NQ * NS, DI], F32, **dk).ap()
    prow_d = nc.dram_tensor("prow_d", [NE, T], BF16, **dk).ap()
    ccin = nc.dram_tensor("ccin", [NS + 1, DI], F32).ap()
    ccout = nc.dram_tensor("ccout", [NCORE * (NS + 1), DI], F32).ap()
    for n_ in ("xn_d", "xct_d", "hst_d", "prow_d", "ccin", "ccout", "yfin"):
        RES[n_] = Res()

    layers = {}

    def sb(name, p, wdt, dt=F32, layer="perm", barrier=()):
        if layer not in layers:
            layers[layer] = contextlib.ExitStack()
        uname = name if layer == "perm" else f"{name}__{layer}"
        t = layers[layer].enter_context(nc.sbuf_tensor(uname, [p, wdt], dt))
        RES[name] = Res(barrier)
        return t

    def retire(*names):
        toks = []
        for n_ in names:
            res = RES.pop(n_)
            if res.w is not None:
                toks.append(res.w)
            toks.extend(res.r)
        return toks

    def close_layer(layer):
        layers.pop(layer).close()

    y = sb("y", 128, DC * T)
    ones_f = sb("ones_f", 128, 128)
    ones_b = sb("ones_b", 128, 128, BF16)
    identb = sb("identb", 128, 128, BF16)
    identf = sb("identf", 128, 128)
    masktri = sb("masktri", 128, 128)
    convb_s = sb("convb_s", 128, CC)
    gw_s = sb("gw_s", 128, DC * NE, BF16)
    abc_s = sb("abc_s", 128, NS)
    a_col = sb("a_col", NS, 1)
    rows16 = sb("rows16", 1, 64)
    srow8 = sb("srow8", 1, 8)
    scol8 = sb("scol8", NE, 1)
    g8 = sb("g8", NE, 1)
    Dt16 = sb("Dt16", NS, NS)
    Pt16 = sb("Pt16", NS, NS)
    coef16 = sb("coef16", NS, NE)
    selbc = sb("selbc", NS, NE)
    gmaskt = sb("gmaskt", NE, NE)
    xnh = sb("xnh", 128, DC * HALO, BF16)
    cst2 = sb("cst2", 1, 2)
    rows_b = sb("rows_b", 1, 1024, BF16)
    xnblk = sb("xnblk", 128, DC * TB, BF16)
    xnblk2 = sb("xnblk2", 128, DC * TB, BF16)

    def pst(name, p, wdt, dt=F32, layer="psum", barrier=()):
        if layer not in layers:
            layers[layer] = contextlib.ExitStack()
        t = layers[layer].enter_context(nc.psum_tensor(name, [p, wdt], dt))
        RES[name] = Res(barrier)
        return t

    pA = pst("pA", 128, 512)
    pB = pst("pB", 128, 512)
    pD = pst("pD", 128, 512)
    pS = pst("pS", 128, 512)
    pBc = pst("pBc", 128, 512)
    pBc2 = pst("pBc2", 128, 512)
    pTrA = pst("pTrA", 128, 512, BF16, layer="psumT")
    pTrB = pst("pTrB", 128, 512, BF16, layer="psumT")

    def dma(queue, dst, src, r=(), w=(), label="dma"):
        return op(queue, lambda e, d=dst, s=src: e.dma_start(out=d, in_=s),
                  r=r, w=w, kind="dma", label=label)

    def mm(out_ap, lhsT_ap, rhs_ap, start, stop, r=(), w=(), label="mm"):
        return op("pe", lambda e: e.matmul(out_ap, lhsT_ap, rhs_ap,
                                           start=start, stop=stop),
                  r=r, w=w, label=label)

    def act(out_ap, in_ap, func, r=(), w=(), scale=None, bias=None,
            label="act"):
        kw = {}
        if scale is not None:
            kw["scale"] = scale
        if bias is not None:
            kw["bias"] = bias
        return op("ac", lambda e: e.activation(out_ap, in_ap, func, **kw),
                  r=r, w=w, label=label)

    dma("sy", ones_f[:, :], ones2d_i, w=[R("ones_f")])
    dma("pl", ones_b[:, :], ones2d_i, w=[R("ones_b")])
    dma("pl", identb[:, :], identf_i, w=[R("identb")])
    dma("sy", identf[:, :], identf_i, w=[R("identf")])
    dma("sy", masktri[:, :], masktri_i, w=[R("masktri")])
    dma("sy", convb_s[:, :], convb_i, w=[R("convb_s")])
    dma("pl", gw_s[:, :].rearrange("p (dc e) -> p dc e", e=NE),
        gwt.rearrange("(dc p) e -> p dc e", p=128), w=[R("gw_s")])
    dma("sy", selbc[:, :], selbc_i, w=[R("selbc")])
    dma("sy", gmaskt[:, :], gmaskt_i, w=[R("gmaskt")])
    dma("sy", rows16[0:1, 48:64], alog_i, w=[R("rows16")])

    act(rows16[0:1, 32:48], rows16[0:1, 48:64], AF.Exp,
        r=[R("rows16")], w=[R("rows16")], label="a_exp")
    op("dv", lambda e: e.tensor_scalar_mul(rows16[0:1, 32:48],
                                           rows16[0:1, 32:48], -1.0),
       r=[R("rows16")], w=[R("rows16")], label="a_neg")
    op("dv", lambda e: e.memset(cst2[0:1, 0:1], EPS), w=[R("cst2")])
    op("dv", lambda e: e.memset(cst2[0:1, 1:2], 1.0), w=[R("cst2")])
    dma("pl", a_col[:, :], rows16[0:1, 32:48], r=[R("rows16")],
        w=[R("a_col")], label="a_col")
    mm(pS[:, 0:NS], ones_f[0:1, :], rows16[0:1, 32:48], True, True,
       r=[R("ones_f"), R("rows16")], w=[R("pS")], label="a_bc")
    op("dv", lambda e: e.tensor_copy(abc_s[:, :], pS[:, 0:NS]),
       r=[R("pS")], w=[R("abc_s")], label="a_bcev")

    yv = y[:, :].rearrange("p (dc t) -> p dc t", t=T)
    xnv_d = xn_d[:, :].rearrange("p (dc t) -> p dc t", t=T)
    xnbufs = [(xnblk, R("xnblk")), (xnblk2, R("xnblk2"))]

    for blk in range(NB):
        src = xt[:, HALO + blk * TB: HALO + (blk + 1) * TB]
        src = src.rearrange("(dc p) t -> p dc t", p=128)
        dma("sy", yv[:, :, blk * TB:(blk + 1) * TB], src,
            w=[R("y")], label=f"xld{blk}")

    xb_bf = sb("xb_bf", 128, DC * TB, BF16, layer="E0")
    sq_bf = sb("sq_bf", 128, DC * TB, BF16, layer="E0")
    sc_all = sb("sc_all", NE, T, layer="E0")
    pe8T_all = sb("pe8T_all", NE, T, BF16, layer="E0")
    rowsA = sb("rowsA", 1, 4 * TB, layer="E0")
    acc_f = sb("acc_f", 128, TB, layer="E0")
    acc_f2 = sb("acc_f2", 128, TB, layer="E0")
    m1_all = sb("m1_all", 128, NQ, layer="E0")
    m2_all = sb("m2_all", 128, NQ, layer="E0")
    ge8 = sb("ge8", 128, 2 * NE, layer="E0")
    p12 = sb("p12", 128, 2 * NQ, layer="E0")
    s8_all = sb("s8_all", 128, NQ * NE, layer="E0")

    dma("sy", acc_f[:, 0:DC * HALO].rearrange("p (dc t) -> p dc t", t=HALO),
        xt[:, 0:HALO].rearrange("(dc p) t -> p dc t", p=128),
        w=[R("acc_f")], label="haloload")
    op("dv", lambda e: e.tensor_copy(xb_bf[:, 0:DC * HALO],
                                     acc_f[:, 0:DC * HALO]),
       r=[R("acc_f")], w=[R("xb_bf")], label="halocast")
    act(sq_bf[:, 0:DC * HALO], xb_bf[:, 0:DC * HALO], AF.Square,
        r=[R("xb_bf")], w=[R("sq_bf")], label="halosq")
    for dc in range(DC):
        mm(pS[0:1, 0:HALO], ones_b[:, 0:1],
           xb_bf[:, dc * HALO:(dc + 1) * HALO], dc == 0, dc == DC - 1,
           r=[R("ones_b"), R("xb_bf")], w=[R("pS")], label=f"hm{dc}")
    for dc in range(DC):
        mm(pBc2[0:1, 0:HALO], ones_b[:, 0:1],
           sq_bf[:, dc * HALO:(dc + 1) * HALO], dc == 0, dc == DC - 1,
           r=[R("ones_b"), R("sq_bf")], w=[R("pBc2")], label=f"hv{dc}")
    op("dv", lambda e: e.tensor_scalar_mul(rowsA[0:1, 0:HALO],
                                           pS[0:1, 0:HALO], 1.0 / D),
       r=[R("pS")], w=[R("rowsA")], label="hm2")
    op("dv", lambda e: e.tensor_scalar_mul(rowsA[0:1, 8:8 + HALO],
                                           pBc2[0:1, 0:HALO], 1.0 / D),
       r=[R("pBc2")], w=[R("rowsA")], label="he2")
    op("dv", lambda e: e.tensor_mul(rowsA[0:1, 16:16 + HALO],
                                    rowsA[0:1, 0:HALO],
                                    rowsA[0:1, 0:HALO]),
       r=[R("rowsA")], w=[R("rowsA")], label="hmsq")
    op("dv", lambda e: e.tensor_sub(rowsA[0:1, 8:8 + HALO],
                                    rowsA[0:1, 8:8 + HALO],
                                    rowsA[0:1, 16:16 + HALO]),
       r=[R("rowsA")], w=[R("rowsA")], label="hvar")
    act(rowsA[0:1, 8:8 + HALO], rowsA[0:1, 8:8 + HALO],
        AF.Ln, bias=cst2[0:1, 0:1], r=[R("rowsA"), R("cst2")],
        w=[R("rowsA")], label="hln")
    act(rowsA[0:1, 8:8 + HALO], rowsA[0:1, 8:8 + HALO],
        AF.Exp, scale=-0.5, r=[R("rowsA")], w=[R("rowsA")], label="hrstd")
    op("dv", lambda e: e.tensor_copy(rows_b[0:1, 0:HALO],
                                     rowsA[0:1, 0:HALO]),
       r=[R("rowsA")], w=[R("rows_b")], label="hmbf")
    op("dv", lambda e: e.tensor_copy(rows_b[0:1, 8:8 + HALO],
                                     rowsA[0:1, 8:8 + HALO]),
       r=[R("rowsA")], w=[R("rows_b")], label="hrbf")
    mm(pBc[:, 0:HALO], ones_b[0:1, :], rows_b[0:1, 0:HALO], True, True,
       r=[R("ones_b"), R("rows_b")], w=[R("pBc")], label="hmbc")
    mm(pBc2[:, 0:HALO], ones_b[0:1, :], rows_b[0:1, 8:8 + HALO],
       True, True, r=[R("ones_b"), R("rows_b")], w=[R("pBc2")], label="hrbc")
    for dc in range(DC):
        op("dv", lambda e, dc=dc: e.tensor_sub(
            acc_f2[:, dc * HALO:(dc + 1) * HALO],
            acc_f[:, dc * HALO:(dc + 1) * HALO], pBc[:, 0:HALO]),
            r=[R("acc_f"), R("pBc")], w=[R("acc_f2")], label=f"hxs{dc}")
        op("dv", lambda e, dc=dc: e.tensor_mul(
            xnh[:, dc * HALO:(dc + 1) * HALO],
            acc_f2[:, dc * HALO:(dc + 1) * HALO], pBc2[:, 0:HALO]),
            r=[R("acc_f2"), R("pBc2")], w=[R("xnh")], label=f"hxn{dc}")

    for blk in range(NB):
        xnb_t, xnb_r = xnbufs[blk % 2]
        for dc in range(DC):
            op("dv", lambda e, dc=dc, blk=blk: e.tensor_copy(
                xb_bf[:, dc * TB:(dc + 1) * TB],
                y[:, dc * T + blk * TB: dc * T + (blk + 1) * TB]),
                r=[R("y")], w=[R("xb_bf")], label=f"cast{blk}_{dc}")
        for dc in range(DC):
            act(sq_bf[:, dc * TB:(dc + 1) * TB],
                xb_bf[:, dc * TB:(dc + 1) * TB], AF.Square,
                r=[R("xb_bf")], w=[R("sq_bf")], label=f"sq{blk}_{dc}")
        for dc in range(DC):
            mm(pS[0:1, 0:512], ones_b[:, 0:1],
               xb_bf[:, dc * TB:(dc + 1) * TB], dc == 0, dc == DC - 1,
               r=[R("ones_b"), R("xb_bf")], w=[R("pS")], label=f"ms{blk}_{dc}")
        for dc in range(DC):
            mm(pBc2[0:1, 0:512], ones_b[:, 0:1],
               sq_bf[:, dc * TB:(dc + 1) * TB], dc == 0, dc == DC - 1,
               r=[R("ones_b"), R("sq_bf")], w=[R("pBc2")],
               label=f"vs{blk}_{dc}")
        op("dv", lambda e: e.tensor_scalar_mul(rowsA[0:1, 0:TB],
                                               pS[0:1, 0:512], 1.0 / D),
           r=[R("pS")], w=[R("rowsA")], label=f"m{blk}")
        op("dv", lambda e: e.tensor_scalar_mul(rowsA[0:1, TB:2 * TB],
                                               pBc2[0:1, 0:512], 1.0 / D),
           r=[R("pBc2")], w=[R("rowsA")], label=f"e2{blk}")
        op("dv", lambda e: e.tensor_mul(rowsA[0:1, 2 * TB:3 * TB],
                                        rowsA[0:1, 0:TB], rowsA[0:1, 0:TB]),
           r=[R("rowsA")], w=[R("rowsA")], label=f"msq{blk}")
        op("dv", lambda e: e.tensor_sub(rowsA[0:1, TB:2 * TB],
                                        rowsA[0:1, TB:2 * TB],
                                        rowsA[0:1, 2 * TB:3 * TB]),
           r=[R("rowsA")], w=[R("rowsA")], label=f"var{blk}")
        act(rowsA[0:1, TB:2 * TB], rowsA[0:1, TB:2 * TB], AF.Ln,
            bias=cst2[0:1, 0:1], r=[R("rowsA"), R("cst2")], w=[R("rowsA")],
            label=f"lnv{blk}")
        act(rowsA[0:1, TB:2 * TB], rowsA[0:1, TB:2 * TB], AF.Exp, scale=-0.5,
            r=[R("rowsA")], w=[R("rowsA")], label=f"rstd{blk}")
        op("dv", lambda e: e.tensor_copy(rows_b[0:1, 0:512],
                                         rowsA[0:1, 0:TB]),
           r=[R("rowsA")], w=[R("rows_b")], label=f"mbf{blk}")
        op("dv", lambda e: e.tensor_copy(rows_b[0:1, 512:1024],
                                         rowsA[0:1, TB:2 * TB]),
           r=[R("rowsA")], w=[R("rows_b")], label=f"rbf{blk}")
        mm(pBc[:, :], ones_b[0:1, :], rows_b[0:1, 0:512], True, True,
           r=[R("ones_b"), R("rows_b")], w=[R("pBc")], label=f"mbc{blk}")
        mm(pBc2[:, :], ones_b[0:1, :], rows_b[0:1, 512:1024], True, True,
           r=[R("ones_b"), R("rows_b")], w=[R("pBc2")], label=f"rbc{blk}")
        accbufs = [(acc_f, R("acc_f")), (acc_f2, R("acc_f2"))]
        for dc in range(DC):
            ab, ar = accbufs[dc % 2]
            op("dv", lambda e, dc=dc, blk=blk, ab=ab: e.tensor_sub(
                ab[:, :], y[:, dc * T + blk * TB: dc * T + (blk + 1) * TB],
                pBc[:, :]),
                r=[R("y"), R("pBc")], w=[ar], label=f"xs{blk}_{dc}")
            op("dv", lambda e, dc=dc, ab=ab, xnb_t=xnb_t: e.tensor_mul(
                xnb_t[:, dc * TB:(dc + 1) * TB], ab[:, :], pBc2[:, :]),
                r=[ar, R("pBc2")], w=[xnb_r], label=f"xn{blk}_{dc}")
        dma("sy", xnv_d[:, :, blk * TB:(blk + 1) * TB],
            xnb_t[:, :].rearrange("p (dc t) -> p dc t", t=TB),
            r=[xnb_r], w=[R("xn_d")], label=f"xnst{blk}")
        for dc in range(DC):
            mm(pD[0:NE, 0:512], gw_s[:, dc * NE:(dc + 1) * NE],
               xnb_t[:, dc * TB:(dc + 1) * TB], dc == 0, dc == DC - 1,
               r=[R("gw_s"), xnb_r], w=[R("pD")], label=f"gsc{blk}_{dc}")
        op("dv", lambda e, blk=blk: e.tensor_copy(
            sc_all[:, blk * TB:(blk + 1) * TB], pD[0:NE, 0:512]),
            r=[R("pD")], w=[R("sc_all")], label=f"gscev{blk}")

    for q in range(NQ):
        op("pe", lambda e, q=q: e.transpose(
            pBc[:, 0:NE], sc_all[:, q * 128:(q + 1) * 128],
            identf[0:NE, 0:NE]),
           r=[R("sc_all"), R("identf")], w=[R("pBc")], label=f"scT{q}")
        op("dv", lambda e, q=q: e.tensor_copy(
            s8_all[:, q * NE:(q + 1) * NE], pBc[:, 0:NE]),
           r=[R("pBc")], w=[R("s8_all")], label=f"scTe{q}")
    for q in range(NQ):
        s8 = s8_all[:, q * NE:(q + 1) * NE]
        op("dv", lambda e, s8=s8, q=q: e.reduce_max(
            m1_all[:, q:q + 1], s8, axis=AX.X),
           r=[R("s8_all")], w=[R("m1_all")], label=f"m1_{q}")
        op("dv", lambda e, s8=s8, q=q: e.tensor_scalar(
            ge8[:, 0:NE], s8, m1_all[:, q:q + 1], None, ALU.is_ge),
           r=[R("s8_all"), R("m1_all")], w=[R("ge8")], label=f"ge1_{q}")
        op("dv", lambda e, s8=s8: e.scalar_tensor_tensor(
            ge8[:, NE:2 * NE], ge8[:, 0:NE], -BIG, s8, ALU.mult, ALU.add),
           r=[R("s8_all"), R("ge8")], w=[R("ge8")], label=f"sm_{q}")
        op("dv", lambda e, q=q: e.reduce_max(
            m2_all[:, q:q + 1], ge8[:, NE:2 * NE], axis=AX.X),
           r=[R("ge8")], w=[R("m2_all")], label=f"m2_{q}")
        op("dv", lambda e, s8=s8, q=q: e.tensor_scalar(
            ge8[:, NE:2 * NE], s8, m2_all[:, q:q + 1], None, ALU.is_ge),
           r=[R("s8_all"), R("m2_all")], w=[R("ge8")], label=f"ge2_{q}")
        op("dv", lambda e: e.tensor_sub(
            ge8[:, NE:2 * NE], ge8[:, NE:2 * NE], ge8[:, 0:NE]),
           r=[R("ge8")], w=[R("ge8")], label=f"eq2_{q}")
        op("dv", lambda e, q=q: e.tensor_sub(
            p12[:, q:q + 1], m1_all[:, q:q + 1], m2_all[:, q:q + 1]),
           r=[R("m1_all"), R("m2_all")], w=[R("p12")], label=f"d12_{q}")
        act(p12[:, q:q + 1], p12[:, q:q + 1], AF.Sigmoid,
            r=[R("p12")], w=[R("p12")], label=f"p1_{q}")
        op("dv", lambda e, q=q: e.tensor_scalar(
            p12[:, NQ + q:NQ + q + 1], p12[:, q:q + 1], -1.0, 1.0,
            ALU.mult, ALU.add),
           r=[R("p12")], w=[R("p12")], label=f"p2_{q}")
        op("dv", lambda e, q=q: e.tensor_scalar(
            ge8[:, 0:NE], ge8[:, 0:NE], p12[:, q:q + 1], None, ALU.mult),
           r=[R("ge8"), R("p12")], w=[R("ge8")], label=f"u_{q}")
        op("dv", lambda e, s8=s8, q=q: e.scalar_tensor_tensor(
            s8, ge8[:, NE:2 * NE], p12[:, NQ + q:NQ + q + 1], ge8[:, 0:NE],
            ALU.mult, ALU.add),
           r=[R("ge8"), R("p12")], w=[R("s8_all")], label=f"pe8_{q}")
        op("pe", lambda e, s8=s8, q=q: e.transpose(
            pBc2[0:NE, 0:128], s8, identf[:, :]),
           r=[R("s8_all"), R("identf")], w=[R("pBc2")], label=f"pT{q}")
        op("dv", lambda e, q=q: e.tensor_copy(
            pe8T_all[:, q * 128:(q + 1) * 128], pBc2[0:NE, 0:128]),
           r=[R("pBc2")], w=[R("pe8T_all")], label=f"pTe{q}")
    dma("sy", prow_d, pe8T_all[:, :], r=[R("pe8T_all")], w=[R("prow_d")],
        label="prowst")

    bar0 = retire("xb_bf", "sq_bf", "sc_all", "pe8T_all", "rowsA",
                  "acc_f", "acc_f2", "m1_all", "m2_all", "ge8", "p12",
                  "s8_all")
    close_layer("E0")
    bar1a = bar0

    if P3_ON:
        call = sb("call", NS, T, BF16, layer="E1s", barrier=bar0)
        lhsT2 = sb("lhsT2", 2, T, layer="E1s", barrier=bar0)
        rhs2 = sb("rhs2", 2, T, layer="E1s", barrier=bar0)
        qt_all = sb("qt_all", NS, T, BF16, layer="E1s", barrier=bar0)
        wstr_a = sb("wstr_a", 128, DC * TB, BF16, layer="E1s", barrier=bar0)
        wstr_b = sb("wstr_b", 128, DC * TB, BF16, layer="E1s", barrier=bar0)
        xcT_s = sb("xcT_s", 128, DI, BF16, layer="E1s", barrier=bar0)
        xcT_s2 = sb("xcT_s2", 128, DI, BF16, layer="E1s", barrier=bar0)
        btT_all = sb("btT_all", 128, NQ * NS, layer="E1s", barrier=bar0)
        khT_all = sb("khT_all", 128, NQ * NS, BF16, layer="E1s",
                     barrier=bar0)
        dsm = sb("dsm", 128, 128, layer="E1s", barrier=bar0)
        dsm2 = sb("dsm2", 128, 128, layer="E1s", barrier=bar0)
        en_a = sb("en_a", 128, 128, layer="E1s", barrier=bar0)
        en_b = sb("en_b", 128, 128, layer="E1s", barrier=bar0)
        en_c = sb("en_c", 128, 128, layer="E1s", barrier=bar0)
        en_d = sb("en_d", 128, 128, layer="E1s", barrier=bar0)
        mt_f = sb("mt_f", 128, 128, layer="E1s", barrier=bar0)
        mt_f2 = sb("mt_f2", 128, 128, layer="E1s", barrier=bar0)
        mt_t = sb("mt_t", 128, 128, layer="E1s", barrier=bar0)
        mt_t2 = sb("mt_t2", 128, 128, layer="E1s", barrier=bar0)
        mtb = sb("mtb", 128, 128, BF16, layer="E1s", barrier=bar0)
        mtb2 = sb("mtb2", 128, 128, BF16, layer="E1s", barrier=bar0)
        delta_row = sb("delta_row", 1, T, layer="E1c", barrier=bar0)
        ball = sb("ball", NS, T, layer="E1c", barrier=bar0)
        xi_blk = sb("xi_blk", 128, CC * XW, BF16, layer="E1a", barrier=bar0)
        for cc in range(CC):
            RES[f"xi{cc}"] = Res(bar0)
        xcT_w = sb("xcT_w", 128, DI, BF16, layer="E1a", barrier=bar0)
        xcT_w2 = sb("xcT_w2", 128, DI, BF16, layer="E1a", barrier=bar0)
        convw_s = sb("convw_s", 128, CC * 4, layer="E1a", barrier=bar0)
        xpw_s = sb("xpw_s", 128, CC * 80, BF16, layer="E1a", barrier=bar0)
        cacc = sb("cacc", 128, TB, BF16, layer="E1a", barrier=bar0)
        cacc2 = sb("cacc2", 128, TB, BF16, layer="E1a", barrier=bar0)
        rows3 = sb("rows3", 1, 4 * TB, layer="E1a", barrier=bar0)

        dma("sy", convw_s[:, :], convw_i, w=[R("convw_s")])
        dma("pl", xpw_s[:, :].rearrange("p (cc w) -> p cc w", w=80),
            xpwt.rearrange("(cc p) w -> p cc w", p=128), w=[R("xpw_s")])
        op("dv", lambda e: e.memset(lhsT2[0:2, :], 1.0), w=[R("lhsT2")])
        op("dv", lambda e: e.memset(rhs2[0:2, :], 1.0), w=[R("rhs2")])

        wbufs = [(wstr_a, R("wstr_a")), (wstr_b, R("wstr_b"))]
        xcwbufs = [(xcT_w, R("xcT_w")), (xcT_w2, R("xcT_w2"))]
        accbufs = [(cacc, R("cacc")), (cacc2, R("cacc2"))]
        inpsum = [(pA, R("pA")), (pB, R("pB"))]

        for blk in range(NB):
            xnb_t, xnb_r = xnbufs[blk % 2]
            dma("sy", xnb_t[:, :].rearrange("p (dc t) -> p dc t", t=TB),
                xnv_d[:, :, blk * TB:(blk + 1) * TB],
                r=[R("xn_d")], w=[xnb_r], label=f"p3xn{blk}")
            def ld_ipx(g, blk=blk):
                wb_t, wb_r = wbufs[(blk * 4 + g) % 2]
                dma("pl", wb_t[:, :].rearrange("p (dc h) -> p dc h", h=TB),
                    ipwt_x[:, g * 512:(g + 1) * 512].rearrange(
                        "(dc p) h -> p dc h", p=128),
                    w=[wb_r], label=f"wipx{blk}_{g}")
            ld_ipx(0)
            ld_ipx(1)
            for g in range(4):
                wb_t, wb_r = wbufs[(blk * 4 + g) % 2]
                for cci in range(4):
                    cc = g * 4 + cci
                    pp_t, pp_r = inpsum[cc % 2]
                    for dc in range(DC):
                        mm(pp_t[:, :], wb_t[:, dc * TB + cci * 128:
                                            dc * TB + (cci + 1) * 128],
                           xnb_t[:, dc * TB:(dc + 1) * TB],
                           dc == 0, dc == DC - 1,
                           r=[wb_r, xnb_r], w=[pp_r],
                           label=f"ipx{blk}_{cc}_{dc}")
                    if blk == 0:
                        for dc in range(DC):
                            mm(pD[:, 0:HALO],
                               wb_t[:, dc * TB + cci * 128:
                                    dc * TB + (cci + 1) * 128],
                               xnh[:, dc * HALO:(dc + 1) * HALO],
                               dc == 0, dc == DC - 1,
                               r=[wb_r, R("xnh")], w=[R("pD")],
                               label=f"iph{cc}_{dc}")
                        op("dv", lambda e, cc=cc: e.tensor_copy(
                            xi_blk[:, cc * XW: cc * XW + HALO], pD[:, 0:HALO]),
                            r=[R("pD")], w=[R(f"xi{cc}")], label=f"iphe{cc}")
                    act(xi_blk[:, cc * XW + HALO: cc * XW + XW],
                        pp_t[:, :], AF.Copy, r=[pp_r], w=[R(f"xi{cc}")],
                        label=f"ipxe{blk}_{cc}")
                if g < 2:
                    ld_ipx(g + 2)

            for cc in range(CC):
                ab, ar = accbufs[cc % 2]
                base = cc * XW
                op("dv", lambda e, cc=cc, ab=ab, base=base: e.tensor_scalar(
                    ab[:, :], xi_blk[:, base: base + TB],
                    convw_s[:, cc * 4: cc * 4 + 1],
                    convb_s[:, cc: cc + 1], ALU.mult, ALU.add),
                   r=[R(f"xi{cc}"), R("convw_s"), R("convb_s")], w=[ar],
                   label=f"cv0_{blk}_{cc}")
                for k in range(1, 4):
                    op("dv", lambda e, k=k, ab=ab, base=base, cc=cc:
                       e.scalar_tensor_tensor(
                           ab[:, :], xi_blk[:, base + k: base + k + TB],
                           convw_s[:, cc * 4 + k: cc * 4 + k + 1], ab[:, :],
                           ALU.mult, ALU.add),
                       r=[R(f"xi{cc}"), R("convw_s"), ar], w=[ar],
                       label=f"cv{k}_{blk}_{cc}")
                if blk < NB - 1:
                    op("dv", lambda e, base=base: e.tensor_copy(
                        xi_blk[:, base: base + HALO],
                        xi_blk[:, base + TB: base + TB + HALO]),
                       r=[R(f"xi{cc}")], w=[R(f"xi{cc}")],
                       label=f"tail{blk}_{cc}")
                act(xi_blk[:, base + HALO: base + XW], ab[:, :], AF.Silu,
                    r=[ar], w=[R(f"xi{cc}")], label=f"silu{blk}_{cc}")

            for cc in range(CC):
                mm(pD[0:80, 0:512], xpw_s[:, cc * 80:(cc + 1) * 80],
                   xi_blk[:, cc * XW + HALO: cc * XW + XW],
                   cc == 0, cc == CC - 1,
                   r=[R("xpw_s"), R(f"xi{cc}")], w=[R("pD")],
                   label=f"xp{blk}_{cc}")
            op("dv", lambda e, blk=blk: e.tensor_copy(
                ball[:, blk * TB:(blk + 1) * TB], pD[0:16, 0:512]),
                r=[R("pD")], w=[R("ball")], label=f"ballev{blk}")
            op("dv", lambda e, blk=blk: e.tensor_copy(
                call[:, blk * TB:(blk + 1) * TB], pD[32:48, 0:512]),
                r=[R("pD")], w=[R("call")], label=f"callev{blk}")
            act(rows3[0:1, 2 * TB:3 * TB], pD[64:65, 0:512], AF.Relu,
                r=[R("pD")], w=[R("rows3")], label=f"sp_r{blk}")
            act(rows3[0:1, 3 * TB:4 * TB], pD[64:65, 0:512], AF.Abs,
                r=[R("pD")], w=[R("rows3")], label=f"sp_a{blk}")
            act(rows3[0:1, 3 * TB:4 * TB], rows3[0:1, 3 * TB:4 * TB], AF.Exp,
                scale=-1.0, r=[R("rows3")], w=[R("rows3")], label=f"sp_e{blk}")
            act(rows3[0:1, 3 * TB:4 * TB], rows3[0:1, 3 * TB:4 * TB], AF.Ln,
                bias=cst2[0:1, 1:2], r=[R("rows3"), R("cst2")],
                w=[R("rows3")], label=f"sp_l{blk}")
            op("dv", lambda e, blk=blk, rows3=rows3: e.tensor_add(
                delta_row[0:1, blk * TB:(blk + 1) * TB],
                rows3[0:1, 2 * TB:3 * TB], rows3[0:1, 3 * TB:4 * TB]),
               r=[R("rows3")], w=[R("delta_row")], label=f"sp_d{blk}")

            for qi in range(QPB):
                q = blk * QPB + qi
                xw_t, xw_r = xcwbufs[q % 2]
                for g4 in range(4):
                    bank_t = pTrA if g4 % 2 == 0 else pTrB
                    bank_r = R("pTrA") if g4 % 2 == 0 else R("pTrB")
                    for cci in range(4):
                        cc = g4 * 4 + cci
                        op("pe", lambda e, cc=cc, qi=qi, cci=cci,
                           bank_t=bank_t: e.transpose(
                               bank_t[:, cci * 128:(cci + 1) * 128],
                               xi_blk[:, cc * XW + HALO + qi * 128:
                                      cc * XW + HALO + (qi + 1) * 128],
                               identb[:, :]),
                           r=[R(f"xi{cc}"), R("identb")], w=[bank_r],
                           label=f"xct{q}_{cc}")
                    op("dv", lambda e, g4=g4, xw_t=xw_t, bank_t=bank_t:
                       e.tensor_copy(xw_t[:, g4 * 512:(g4 + 1) * 512],
                                     bank_t[:, :]),
                       r=[bank_r], w=[xw_r], label=f"xcte{q}_{g4}")
                dma("sy", xct_d[q * 128:(q + 1) * 128, :], xw_t[:, :],
                    r=[xw_r], w=[R("xct_d")], label=f"xctst{q}")

        bar1a = retire("xi_blk", "xcT_w", "xcT_w2", "convw_s", "xpw_s",
                       "cacc", "cacc2", "rows3",
                       *[f"xi{cc}" for cc in range(CC)])
        close_layer("E1a")

    bar2 = bar1a
    if P4_ON and P3_ON:
        btilde = sb("btilde", NS, T, layer="E1b", barrier=bar1a)
        khat = sb("khat", NS, T, BF16, layer="E1b", barrier=bar1a)
        efk = sb("efk", NS, T, layer="E1b", barrier=bar1a)
        hw_a = sb("hw_a", NS, DI, layer="E1b", barrier=bar1a)
        hw_b = sb("hw_b", NS, DI, layer="E1b", barrier=bar1a)
        rowsB = sb("rowsB", 1, T, layer="E1b", barrier=bar1a)
        ch16 = sb("ch16", NS, 128, layer="E1b", barrier=bar1a)
        csch = sb("csch", NS, 128, layer="E1b", barrier=bar1a)
        negch = sb("negch", NS, 128, layer="E1b", barrier=bar1a)
        skch = sb("skch", NS, 128, layer="E1b", barrier=bar1a)

        op("dv", lambda e: e.memset(hw_a[:, :], 0.0), w=[R("hw_a")])

        dma("pl", ch16[:, :], delta_row[0:1, :], r=[R("delta_row")],
            w=[R("ch16")], label="dch")
        op("dv", lambda e: e.tensor_tensor_scan(
            csch[:, :], ch16[:, :], ch16[:, :], 0.0, ALU.add, ALU.bypass),
           r=[R("ch16")], w=[R("csch")], label="cumsum")
        op("dv", lambda e: e.tensor_scalar_mul(negch[:, :], csch[:, :], -1.0),
           r=[R("csch")], w=[R("negch")], label="negcs")
        op("dv", lambda e: e.tensor_scalar(
            skch[:, :], csch[:, :], csch[:, 127:128], -1.0,
            ALU.subtract, ALU.mult),
           r=[R("csch")], w=[R("skch")], label="skdiff")
        dma("pl", rhs2[0:1, :], csch[:, :], r=[R("csch")], w=[R("rhs2")],
            label="csrow")
        dma("pl", lhsT2[1:2, :], negch[:, :], r=[R("negch")], w=[R("lhsT2")],
            label="negrow")
        dma("pl", rowsB[0:1, :], skch[:, :], r=[R("skch")], w=[R("rowsB")],
            label="skrow")
        dma("pl", rows16[0:1, 0:16], csch[:, 127:128], r=[R("csch")],
            w=[R("rows16")], label="srow")
        op("dv", lambda e: e.tensor_tensor_scan(
            rows16[0:1, 16:32], rows16[0:1, 0:16], rows16[0:1, 0:16], 0.0,
            ALU.add, ALU.bypass),
           r=[R("rows16")], w=[R("rows16")], label="csS")
        op("dv", lambda e: e.tensor_sub(
            rows16[0:1, 32:48], rows16[0:1, 16:32], rows16[0:1, 0:16]),
           r=[R("rows16")], w=[R("rows16")], label="csx")

        mm(pS[0:NS, 0:NS], ones_f[0:1, 0:NS], rows16[0:1, 0:16], True, True,
           r=[R("ones_f"), R("rows16")], w=[R("pS")], label="sbc16")
        act(Dt16[:, :], pS[0:NS, 0:NS], AF.Exp, scale=a_col[:, 0:1],
            r=[R("pS"), R("a_col")], w=[R("Dt16")], label="dt16")
        mm(pS[0:NS, 0:NS], ones_f[0:1, 0:NS], rows16[0:1, 32:48], True, True,
           r=[R("ones_f"), R("rows16")], w=[R("pS")], label="csbc16")
        act(Pt16[:, :], pS[0:NS, 0:NS], AF.Exp, scale=a_col[:, 0:1],
            r=[R("pS"), R("a_col")], w=[R("Pt16")], label="pt16")

        for g in range(4):
            sl = slice(g * 512, (g + 1) * 512)
            op("dv", lambda e, sl=sl: e.tensor_copy(
                rows_b[0:1, 0:512], delta_row[0:1, sl]),
               r=[R("delta_row")], w=[R("rows_b")], label=f"deltab{g}")
            mm(pBc[0:NS, 0:512], ones_b[0:1, 0:NS], rows_b[0:1, 0:512],
               True, True, r=[R("ones_b"), R("rows_b")], w=[R("pBc")],
               label=f"dbcast{g}")
            op("dv", lambda e, sl=sl: e.tensor_mul(
                btilde[:, sl], ball[:, sl], pBc[0:NS, 0:512]),
               r=[R("ball"), R("pBc")], w=[R("btilde")], label=f"btl{g}")
            mm(pBc2[0:NS, 0:512], ones_f[0:1, 0:NS], rowsB[0:1, sl],
               True, True, r=[R("ones_f"), R("rowsB")], w=[R("pBc2")],
               label=f"skbc{g}")
            act(efk[:, sl], pBc2[0:NS, 0:512], AF.Exp, scale=a_col[:, 0:1],
                r=[R("pBc2"), R("a_col")], w=[R("efk")], label=f"efk{g}")
            op("dv", lambda e, sl=sl: e.tensor_mul(
                khat[:, sl], btilde[:, sl], efk[:, sl]),
               r=[R("btilde"), R("efk")], w=[R("khat")], label=f"kh{g}")
            mm(pBc[0:NS, 0:512], ones_f[0:1, 0:NS], rhs2[0:1, sl], True, True,
               r=[R("ones_f"), R("rhs2")], w=[R("pBc")], label=f"csbc{g}")
            act(efk[:, sl], pBc[0:NS, 0:512], AF.Exp, scale=a_col[:, 0:1],
                r=[R("pBc"), R("a_col")], w=[R("efk")], label=f"efq{g}")
            op("dv", lambda e, sl=sl: e.tensor_mul(
                qt_all[:, sl], call[:, sl], efk[:, sl]),
               r=[R("call"), R("efk")], w=[R("qt_all")], label=f"qt{g}")

        for q in range(NQ):
            op("pe", lambda e, q=q: e.transpose(
                pD[:, 0:NS], btilde[:, q * 128:(q + 1) * 128],
                identf[0:NS, 0:NS]),
               r=[R("btilde"), R("identf")], w=[R("pD")], label=f"btT{q}")
            op("dv", lambda e, q=q: e.tensor_copy(
                btT_all[:, q * NS:(q + 1) * NS], pD[:, 0:NS]),
               r=[R("pD")], w=[R("btT_all")], label=f"btTe{q}")
            op("pe", lambda e, q=q: e.transpose(
                pTrA[:, 0:NS], khat[:, q * 128:(q + 1) * 128],
                identb[0:NS, 0:NS]),
               r=[R("khat"), R("identb")], w=[R("pTrA")], label=f"khT{q}")
            op("dv", lambda e, q=q: e.tensor_copy(
                khT_all[:, q * NS:(q + 1) * NS], pTrA[:, 0:NS]),
               r=[R("pTrA")], w=[R("khT_all")], label=f"khTe{q}")

        xcsbufs = [(xcT_s, R("xcT_s")), (xcT_s2, R("xcT_s2"))]
        hwbufs = [(hw_a, R("hw_a")), (hw_b, R("hw_b"))]
        hpsum = [(pA, R("pA")), (pB, R("pB"))]
        for q in range(NQ):
            xs_t, xs_r = xcsbufs[q % 2]
            dma("sy", xs_t[:, :], xct_d[q * 128:(q + 1) * 128, :],
                r=[R("xct_d")], w=[xs_r], label=f"xcld{q}")
            hw_cur, hwc_r = hwbufs[q % 2]
            hw_nxt, hwn_r = hwbufs[(q + 1) % 2]
            dma("sy", hst_d[q * NS:(q + 1) * NS, :], hw_cur[:, :],
                r=[hwc_r], w=[R("hst_d")], label=f"hstst{q}")
            for g in range(4):
                pp_t, pp_r = hpsum[g % 2]
                mm(pp_t[0:NS, 0:512], khT_all[:, q * NS:(q + 1) * NS],
                   xs_t[:, g * 512:(g + 1) * 512], True, True,
                   r=[R("khT_all"), xs_r], w=[pp_r], label=f"hc{q}_{g}")
                sl = slice(g * 512, (g + 1) * 512)
                op("dv", lambda e, sl=sl, q=q, hw_cur=hw_cur, hw_nxt=hw_nxt:
                   e.tensor_scalar(hw_nxt[:, sl], hw_cur[:, sl],
                                   Dt16[:, q:q + 1], None, ALU.mult),
                   r=[hwc_r, R("Dt16")], w=[hwn_r], label=f"hs{q}_{g}")
                op("dv", lambda e, sl=sl, hw_nxt=hw_nxt, pp_t=pp_t:
                   e.tensor_add(hw_nxt[:, sl], hw_nxt[:, sl],
                                pp_t[0:NS, 0:512]),
                   r=[pp_r, hwn_r], w=[hwn_r], label=f"ha{q}_{g}")
        hend_t, hend_r = hwbufs[NQ % 2]
        dma("sy", ccin[0:NS, :], hend_t[:, :], r=[hend_r], w=[R("ccin")],
            label="ccin_h")
        dma("sy", ccin[NS:NS + 1, :], csch[:, :], r=[R("csch")],
            w=[R("ccin")], label="ccin_fill")
        dma("sy", ccin[NS:NS + 1, 0:1], rows16[0:1, 31:32], r=[R("rows16")],
            w=[R("ccin")], label="ccin_s")
        op("pl", lambda e: e.collective_compute(
            "AllGather", ALU.bypass, replica_groups=[list(range(NCORE))],
            ins=[ccin.opt()], outs=[ccout.opt()]),
           r=[R("ccin")], w=[R("ccout")], kind="cc", label="allgather")

        barT = retire("pTrA", "pTrB")
        close_layer("psumT")
        bar1b = retire("btilde", "khat", "efk", "hw_a", "hw_b", "rowsB",
                       "ch16", "csch", "negch", "skch")
        close_layer("E1b")
        bar1c = retire("delta_row", "ball")
        close_layer("E1c")
        bar2 = bar1b + bar1c

    bar3 = bar2
    if P4_ON and P3_ON:
        pC = pst("pC", 128, 512, layer="psumC", barrier=barT)
    else:
        pC = pst("pC", 128, 512, layer="psumC")
    if P5_ON and P4_ON and P3_ON:
        szg = sb("szg", 128, CC * TB, BF16, layer="E2", barrier=bar2)
        hst_s = sb("hst_s", NS, DI, layer="E2", barrier=bar2)
        htr_b = sb("htr_b", NS, DI, BF16, layer="E2", barrier=bar2)
        hag = sb("hag", NS, DI, layer="E2", barrier=bar2)
        h_in = sb("h_in", NS, DI, layer="E2", barrier=bar2)
        xcdT = sb("xcdT", 128, DI, BF16, layer="E2", barrier=bar2)
        crow = sb("crow", 1, DI, BF16, layer="E2", barrier=bar2)
        crow2 = sb("crow2", 1, DI, BF16, layer="E2", barrier=bar2)
        dbcT = sb("dbcT", 128, DI, BF16, layer="E2", barrier=bar2)

        dma("sy", hag[0:1, :], drow_i, w=[R("hag")], label="drow")
        op("dv", lambda e: e.tensor_copy(crow[0:1, :], hag[0:1, :]),
           r=[R("hag")], w=[R("crow")], label="drowc")
        for g in range(4):
            mm(pBc[:, :], ones_b[0:1, :], crow[0:1, g * 512:(g + 1) * 512],
               True, True, r=[R("ones_b"), R("crow")], w=[R("pBc")],
               label=f"dbc{g}")
            op("dv", lambda e, g=g: e.tensor_copy(
                dbcT[:, g * 512:(g + 1) * 512], pBc[:, :]),
                r=[R("pBc")], w=[R("dbcT")], label=f"dbcev{g}")

        wbufs = [(wstr_a, R("wstr_a")), (wstr_b, R("wstr_b"))]
        inpsum = [(pA, R("pA")), (pB, R("pB"))]
        op("dv", lambda e: e.memset(h_in[:, :], 0.0), w=[R("h_in")])

        zdone = set()

        def emit_zblock(blk):
            xnb_t, xnb_r = xnbufs[blk % 2]
            dma("sy", xnb_t[:, :].rearrange("p (dc t) -> p dc t", t=TB),
                xnv_d[:, :, blk * TB:(blk + 1) * TB],
                r=[R("xn_d")], w=[xnb_r], label=f"zxn{blk}")
            def ld_ipz(g, blk=blk):
                wb_t, wb_r = wbufs[(blk * 4 + g) % 2]
                dma("pl", wb_t[:, :].rearrange("p (dc h) -> p dc h", h=TB),
                    ipwt_z[:, g * 512:(g + 1) * 512].rearrange(
                        "(dc p) h -> p dc h", p=128),
                    w=[wb_r], label=f"wipz{blk}_{g}")
            ld_ipz(0)
            ld_ipz(1)
            for g in range(4):
                wb_t, wb_r = wbufs[(blk * 4 + g) % 2]
                for cci in range(4):
                    cc = g * 4 + cci
                    pp_t, pp_r = inpsum[cc % 2]
                    for dc in range(DC):
                        mm(pp_t[:, :],
                           wb_t[:, dc * TB + cci * 128:
                                dc * TB + (cci + 1) * 128],
                           xnb_t[:, dc * TB:(dc + 1) * TB],
                           dc == 0, dc == DC - 1,
                           r=[wb_r, xnb_r], w=[pp_r],
                           label=f"ipz{blk}_{cc}_{dc}")
                    act(szg[:, cc * TB:(cc + 1) * TB], pp_t[:, :], AF.Silu,
                        r=[pp_r], w=[R("szg")], label=f"zs{blk}_{cc}")
                if g < 2:
                    ld_ipz(g + 2)
            zdone.add(blk)

        emit_zblock(0)

        sg_src = ccout.rearrange("(c r) w -> c (r w)", r=NS + 1)[
            :, NS * DI: NS * DI + 1]
        op("sy", lambda e: e.dma_start(out=srow8[0:1, :], in_=sg_src,
                                       allow_slow_non_contiguous=True),
           r=[R("ccout")], w=[R("srow8")], kind="dma", label="sgather")
        dma("pl", scol8[:, :], srow8[0:1, :], r=[R("srow8")], w=[R("scol8")],
            label="scol8")
        mm(pS[0:NE, 0:1], gmaskt[:, :], scol8[:, :], True, True,
           r=[R("gmaskt"), R("scol8")], w=[R("pS")], label="gmm")
        op("dv", lambda e: e.tensor_copy(g8[:, :], pS[0:NE, 0:1]),
           r=[R("pS")], w=[R("g8")], label="g8ev")
        dma("pl", rows16[0:1, 48:56], g8[:, :], r=[R("g8")], w=[R("rows16")],
            label="grow")
        mm(pS[0:NS, 0:NE], ones_f[0:1, 0:NS], rows16[0:1, 48:56], True, True,
           r=[R("ones_f"), R("rows16")], w=[R("pS")], label="gbc")
        act(coef16[:, :], pS[0:NS, 0:NE], AF.Exp, scale=a_col[:, 0:1],
            r=[R("pS"), R("a_col")], w=[R("coef16")], label="coefexp")
        op("dv", lambda e: e.tensor_mul(coef16[:, :], coef16[:, :],
                                        selbc[:, :]),
           r=[R("coef16"), R("selbc")], w=[R("coef16")], label="coefsel")
        for j in range(NCORE):
            dma("sy", hag[:, :], ccout[j * (NS + 1): j * (NS + 1) + NS, :],
                r=[R("ccout")], w=[R("hag")], label=f"hagld{j}")
            op("dv", lambda e, j=j: e.scalar_tensor_tensor(
                h_in[:, :], hag[:, :], coef16[:, j:j + 1], h_in[:, :],
                ALU.mult, ALU.add),
               r=[R("hag"), R("coef16"), R("h_in")], w=[R("h_in")],
               label=f"hin{j}")

        xcsbufs = [(xcT_s, R("xcT_s")), (xcT_s2, R("xcT_s2"))]
        enbufs = [(en_a, R("en_a")), (en_b, R("en_b"))]
        ypsum = [(pC, R("pC")), (pD, R("pD"))]
        cbcpsum = [(pBc, R("pBc")), (pBc2, R("pBc2"))]
        for blk in range(NB):
            if blk not in zdone:
                emit_zblock(blk)
            for qi in range(QPB):
                q = blk * QPB + qi
                xs_t, xs_r = xcsbufs[q % 2]
                dma("sy", xs_t[:, :], xct_d[q * 128:(q + 1) * 128, :],
                    r=[R("xct_d")], w=[xs_r], label=f"s2xc{q}")
                dma("sy", hst_s[:, :], hst_d[q * NS:(q + 1) * NS, :],
                    r=[R("hst_d")], w=[R("hst_s")], label=f"s2hs{q}")
                mm(pS[:, 0:128], lhsT2[:, q * 128:(q + 1) * 128],
                   rhs2[:, q * 128:(q + 1) * 128], True, True,
                   r=[R("lhsT2"), R("rhs2")], w=[R("pS")], label=f"ds{q}")
                dsm_t, dsm_r = [(dsm, R("dsm")), (dsm2, R("dsm2"))][q % 2]
                mtf_t, mtf_r = [(mt_f, R("mt_f")),
                                (mt_f2, R("mt_f2"))][q % 2]
                mtt_t, mtt_r = [(mt_t, R("mt_t")),
                                (mt_t2, R("mt_t2"))][q % 2]
                mtb_t, mtb_r = [(mtb, R("mtb")), (mtb2, R("mtb2"))][q % 2]
                enbufs_q = [[(en_a, R("en_a")), (en_b, R("en_b"))],
                            [(en_c, R("en_c")), (en_d, R("en_d"))]][q % 2]
                op("dv", lambda e, dsm_t=dsm_t: e.tensor_scalar_max(
                    dsm_t[:, :], pS[:, 0:128], 0.0),
                   r=[R("pS")], w=[dsm_r], label=f"dsm{q}")
                cr_t, cr_r = [(crow, R("crow")), (crow2, R("crow2"))][q % 2]
                dma("pl", cr_t[0:1, :], call[:, q * 128:(q + 1) * 128],
                    r=[R("call")], w=[cr_r], label=f"crow{q}")
                op("dv", lambda e, q=q: e.scalar_tensor_tensor(
                    htr_b[:, :], h_in[:, :], Pt16[:, q:q + 1], hst_s[:, :],
                    ALU.mult, ALU.add),
                   r=[R("h_in"), R("Pt16"), R("hst_s")], w=[R("htr_b")],
                   label=f"htr{q}")
                op("dv", lambda e, xs_t=xs_t: e.tensor_mul(
                    xcdT[:, :], xs_t[:, :], dbcT[:, :]),
                   r=[xs_r, R("dbcT")], w=[R("xcdT")], label=f"xcd{q}")
                for n in range(NS):
                    en_t, en_r = enbufs_q[n % 2]
                    act(en_t[:, :], dsm_t[:, :], AF.Exp,
                        scale=abc_s[:, n:n + 1],
                        r=[dsm_r, R("abc_s")], w=[en_r], label=f"en{q}_{n}")
                    cb_t, cb_r = cbcpsum[(n // 4) % 2]
                    if n % 4 == 0:
                        g2 = n // 4
                        mm(cb_t[:, :], ones_b[0:1, :],
                           cr_t[0:1, g2 * 512:(g2 + 1) * 512], True, True,
                           r=[R("ones_b"), cr_r], w=[cb_r],
                           label=f"cbc{q}_{g2}")
                    csl = slice((n % 4) * 128, (n % 4 + 1) * 128)
                    if n == 0:
                        op("dv", lambda e, en_t=en_t, q=q, cb_t=cb_t, csl=csl,
                           mtf_t=mtf_t: e.scalar_tensor_tensor(
                               mtf_t[:, :], en_t[:, :],
                               btT_all[:, q * NS: q * NS + 1],
                               cb_t[:, csl], ALU.mult, ALU.mult),
                           r=[en_r, R("btT_all"), cb_r], w=[mtf_r],
                           label=f"mt{q}_{n}")
                    else:
                        op("dv", lambda e, en_t=en_t, q=q, n=n, cb_t=cb_t,
                           csl=csl, mtt_t=mtt_t: e.scalar_tensor_tensor(
                               mtt_t[:, :], en_t[:, :],
                               btT_all[:, q * NS + n: q * NS + n + 1],
                               cb_t[:, csl], ALU.mult, ALU.mult),
                           r=[en_r, R("btT_all"), cb_r], w=[mtt_r],
                           label=f"mtt{q}_{n}")
                        op("dv", lambda e, mtf_t=mtf_t, mtt_t=mtt_t:
                           e.tensor_add(
                               mtf_t[:, :], mtf_t[:, :], mtt_t[:, :]),
                           r=[mtt_r, mtf_r], w=[mtf_r],
                           label=f"mta{q}_{n}")
                op("dv", lambda e, mtb_t=mtb_t, mtf_t=mtf_t: e.tensor_mul(
                    mtb_t[:, :], mtf_t[:, :], masktri[:, :]),
                   r=[mtf_r, R("masktri")], w=[mtb_r], label=f"mtb{q}")
                for ccg in range(4):
                    yp_t, yp_r = ypsum[(q * 4 + ccg) % 2]
                    for cci in range(4):
                        cc = ccg * 4 + cci
                        osl = slice(cci * 128, (cci + 1) * 128)
                        mm(yp_t[:, osl], htr_b[:, cc * 128:(cc + 1) * 128],
                           qt_all[:, q * 128:(q + 1) * 128], True, False,
                           r=[R("htr_b"), R("qt_all")], w=[yp_r],
                           label=f"y1_{q}_{cc}")
                        mm(yp_t[:, osl], xs_t[:, cc * 128:(cc + 1) * 128],
                           mtb_t[:, :], False, False,
                           r=[xs_r, mtb_r], w=[yp_r], label=f"y2_{q}_{cc}")
                        mm(yp_t[:, osl], xcdT[:, cc * 128:(cc + 1) * 128],
                           identb[:, :], False, True,
                           r=[R("xcdT"), R("identb")], w=[yp_r],
                           label=f"y3_{q}_{cc}")
                    for cci in range(4):
                        cc = ccg * 4 + cci
                        osl = slice(cci * 128, (cci + 1) * 128)
                        gsl = slice(cc * TB + qi * 128,
                                    cc * TB + (qi + 1) * 128)
                        op("dv", lambda e, yp_t=yp_t, osl=osl, gsl=gsl:
                           e.tensor_mul(szg[:, gsl], yp_t[:, osl],
                                        szg[:, gsl]),
                           r=[yp_r, R("szg")], w=[R("szg")],
                           label=f"gate{q}_{cc}")
            def ld_op(g, blk=blk):
                wb_t, wb_r = wbufs[(blk * 4 + g) % 2]
                dma("pl", wb_t[:, 0:CC * 256].rearrange(
                    "p (cc h) -> p cc h", h=256),
                    opwt[:, g * 256:(g + 1) * 256].rearrange(
                        "(cc p) h -> p cc h", p=128),
                    w=[wb_r], label=f"wop{blk}_{g}")
            ld_op(0)
            ld_op(1)
            for g in range(4):
                wb_t, wb_r = wbufs[(blk * 4 + g) % 2]
                for doi in range(2):
                    do = g * 2 + doi
                    pp_t, pp_r = inpsum[do % 2]
                    for cc in range(CC):
                        mm(pp_t[:, :],
                           wb_t[:, cc * 256 + doi * 128:
                                cc * 256 + (doi + 1) * 128],
                           szg[:, cc * TB:(cc + 1) * TB],
                           cc == 0, cc == CC - 1,
                           r=[wb_r, R("szg")], w=[pp_r],
                           label=f"op{blk}_{do}_{cc}")
                    op("dv", lambda e, do=do, blk=blk, pp_t=pp_t:
                       e.tensor_add(
                           y[:, do * T + blk * TB: do * T + (blk + 1) * TB],
                           y[:, do * T + blk * TB: do * T + (blk + 1) * TB],
                           pp_t[:, :]),
                       r=[pp_r, R("y")], w=[R("y")], label=f"yop{blk}_{do}")
                if g < 2:
                    ld_op(g + 2)

        bar3 = retire("szg", "hst_s", "htr_b", "hag", "h_in",
                      "xcdT", "crow", "dbcT")
        close_layer("E2")
        bar3 = bar3 + retire("call", "lhsT2", "rhs2", "qt_all", "wstr_a",
                             "wstr_b", "xcT_s", "xcT_s2", "btT_all",
                             "khT_all", "dsm", "dsm2", "en_a", "en_b",
                             "en_c", "en_d", "mt_f", "mt_f2", "mt_t",
                             "mt_t2", "mtb", "mtb2")
        close_layer("E1s")

    if P6_ON:
        w12A = sb("w12A", 128, DC * DI, BF16, layer="E3", barrier=bar3)
        w12B = sb("w12B", 128, DC * DI, BF16, layer="E3", barrier=bar3)
        w3_s = sb("w3_s", 128, HK * D, BF16, layer="E3", barrier=bar3)
        hid_blk = sb("hid_blk", 128, HK * TB, BF16, layer="E3", barrier=bar3)
        s1_a = sb("s1_a", 128, TB, layer="E3", barrier=bar3)
        s1_b = sb("s1_b", 128, TB, layer="E3", barrier=bar3)
        t2_a = sb("t2_a", 128, TB, layer="E3", barrier=bar3)
        t2_b = sb("t2_b", 128, TB, layer="E3", barrier=bar3)
        pbc_s = sb("pbc_s", 128, TB, layer="E3", barrier=bar3)

        mm1psum = [[(pA, R("pA")), (pC, R("pC"))],
                   [(pB, R("pB")), (pD, R("pD"))]]
        mm2psum = [(pS, R("pS")), (pBc, R("pBc"))]
        s1bufs = [(s1_a, R("s1_a")), (s1_b, R("s1_b"))]
        t2bufs = [(t2_a, R("t2_a")), (t2_b, R("t2_b"))]

        def load_expert_w(e_i, part):
            if part in ("A", "B"):
                tgt = w12A if part == "A" else w12B
                res = R("w12A") if part == "A" else R("w12B")
                off = 0 if part == "A" else 1024
                dma("pl", tgt[:, :].rearrange(
                    "p (dc h) -> p dc h", h=DI)[:, :, 0:1024],
                    w12t[e_i][:, off:off + 1024].rearrange(
                        "(dc p) h -> p dc h", p=128),
                    w=[res], label=f"w{part}1_{e_i}")
                dma("pl", tgt[:, :].rearrange(
                    "p (dc h) -> p dc h", h=DI)[:, :, 1024:2048],
                    w12t[e_i][:, DI + off:DI + off + 1024].rearrange(
                        "(dc p) h -> p dc h", p=128),
                    w=[res], label=f"w{part}2_{e_i}")
            else:
                dma("pl", w3_s[:, :].rearrange("p (hc h) -> p hc h", h=D),
                    w3t[e_i].rearrange("(hc p) h -> p hc h", p=128),
                    w=[R("w3_s")], label=f"w3_{e_i}")

        load_expert_w(0, "A")
        load_expert_w(0, "B")
        load_expert_w(0, "3")
        for e_i in range(NE):
            for blk in range(NB):
                xnb_t, xnb_r = xnbufs[(e_i * NB + blk) % 2]
                dma("sy", xnb_t[:, :].rearrange("p (dc t) -> p dc t", t=TB),
                    xnv_d[:, :, blk * TB:(blk + 1) * TB],
                    r=[R("xn_d")], w=[xnb_r], label=f"fxn{e_i}_{blk}")
                dma("sy", rows_b[0:1, 0:512],
                    prow_d[e_i:e_i + 1, blk * TB:(blk + 1) * TB],
                    r=[R("prow_d")], w=[R("rows_b")], label=f"fpr{e_i}_{blk}")
                mm(pBc2[:, :], ones_b[0:1, :], rows_b[0:1, 0:512], True, True,
                   r=[R("ones_b"), R("rows_b")], w=[R("pBc2")],
                   label=f"fpb{e_i}_{blk}")
                act(pbc_s[:, :], pBc2[:, :], AF.Copy,
                    r=[R("pBc2")], w=[R("pbc_s")], label=f"fpe{e_i}_{blk}")
                for k in range(HK):
                    wsrc = w12A if k < 8 else w12B
                    wres = R("w12A") if k < 8 else R("w12B")
                    kk = k % 8
                    p1_t, p1_r = mm1psum[0][k % 2]
                    p2_t, p2_r = mm1psum[1][k % 2]
                    for dc in range(DC):
                        mm(p1_t[:, :],
                           wsrc[:, dc * DI + kk * 128:
                                dc * DI + (kk + 1) * 128],
                           xnb_t[:, dc * TB:(dc + 1) * TB],
                           dc == 0, dc == DC - 1,
                           r=[wres, xnb_r], w=[p1_r],
                           label=f"f1_{e_i}_{blk}_{k}_{dc}")
                    for dc in range(DC):
                        mm(p2_t[:, :],
                           wsrc[:, dc * DI + 1024 + kk * 128:
                                dc * DI + 1024 + (kk + 1) * 128],
                           xnb_t[:, dc * TB:(dc + 1) * TB],
                           dc == 0, dc == DC - 1,
                           r=[wres, xnb_r], w=[p2_r],
                           label=f"f2_{e_i}_{blk}_{k}_{dc}")
                    s1_t, s1_r = s1bufs[k % 2]
                    t2_t, t2_r = t2bufs[k % 2]
                    act(s1_t[:, :], p1_t[:, :], AF.Silu,
                        r=[p1_r], w=[s1_r], label=f"fs{e_i}_{blk}_{k}")
                    op("dv", lambda e, p2_t=p2_t, t2_t=t2_t: e.tensor_mul(
                        t2_t[:, :], p2_t[:, :], pbc_s[:, :]),
                       r=[p2_r, R("pbc_s")], w=[t2_r],
                       label=f"ft{e_i}_{blk}_{k}")
                    heng = "pl" if k % 2 == 0 else "dv"
                    op(heng, lambda e, s1_t=s1_t, t2_t=t2_t, k=k:
                       e.tensor_mul(hid_blk[:, k * TB:(k + 1) * TB],
                                    s1_t[:, :], t2_t[:, :]),
                       r=[s1_r, t2_r], w=[R("hid_blk")],
                       label=f"fh{e_i}_{blk}_{k}")
                for do in range(DC):
                    po_t, po_r = mm2psum[do % 2]
                    for k in range(HK):
                        mm(po_t[:, :],
                           w3_s[:, k * D + do * 128: k * D + (do + 1) * 128],
                           hid_blk[:, k * TB:(k + 1) * TB],
                           k == 0, k == HK - 1,
                           r=[R("w3_s"), R("hid_blk")], w=[po_r],
                           label=f"f3_{e_i}_{blk}_{do}_{k}")
                    op("dv", lambda e, do=do, blk=blk, po_t=po_t:
                       e.tensor_add(
                           y[:, do * T + blk * TB: do * T + (blk + 1) * TB],
                           y[:, do * T + blk * TB: do * T + (blk + 1) * TB],
                           po_t[:, :]),
                       r=[po_r, R("y")], w=[R("y")],
                       label=f"fy{e_i}_{blk}_{do}")
                if blk == NB - 1 and e_i + 1 < NE:
                    load_expert_w(e_i + 1, "A")
                    load_expert_w(e_i + 1, "B")
                    load_expert_w(e_i + 1, "3")

    tst = dma("sy", outt.rearrange("(dc p) t -> p dc t", p=128),
              y[:, :].rearrange("p (dc t) -> p dc t", t=T),
              r=[R("y")], w=[R("yfin")], label="ystore")
    b.do("sy", None, after=[tst], kind="nop", label="fence")

    b.finalize_and_emit(nc)
    return nc


_NC_CACHE = {}


def kernel(**inputs):
    x = np.asarray(inputs["L0_attn_x"], np.float32)[0]
    ipw = np.asarray(inputs["L0_attn_in_proj_w"], np.float32)
    convw = np.asarray(inputs["L0_attn_conv_w"], np.float32)[:, 0, :]
    convb = np.asarray(inputs["L0_attn_conv_b"], np.float32)
    xpw = np.asarray(inputs["L0_attn_x_proj_w"], np.float32)
    A_log = np.asarray(inputs["L0_attn_A_log"], np.float32)
    Dsk = np.asarray(inputs["L0_attn_D"], np.float32)
    opw = np.asarray(inputs["L0_attn_out_proj_w"], np.float32)
    w12 = np.asarray(inputs["L0_ffn_w12"], np.float32)
    w3 = np.asarray(inputs["L0_ffn_w3"], np.float32)
    gw = np.asarray(inputs["L0_ffn_gate_w"], np.float32)

    xT = np.ascontiguousarray(x.T)
    ipwt_x = np.ascontiguousarray(ipw[:DI].T)
    ipwt_z = np.ascontiguousarray(ipw[DI:].T)
    xpwt = np.zeros((DI, 80), np.float32)
    xpwt[:, 0:16] = xpw[1:17].T
    xpwt[:, 32:48] = xpw[17:33].T
    xpwt[:, 64] = xpw[0]
    opwt = np.ascontiguousarray(opw.T)
    w12t = np.ascontiguousarray(np.transpose(w12, (0, 2, 1)))
    w3t = np.ascontiguousarray(np.transpose(w3, (0, 2, 1)))
    gwt = np.ascontiguousarray(gw.T)
    convw_p = np.ascontiguousarray(
        convw.reshape(CC, 128, 4).transpose(1, 0, 2).reshape(128, CC * 4))
    convb_p = np.ascontiguousarray(convb.reshape(CC, 128).T)
    drow = np.ascontiguousarray(Dsk.reshape(1, DI))
    alog = np.ascontiguousarray(A_log.reshape(1, NS))
    ones2d = np.ones((128, 128), np.float32)
    identf = np.eye(128, dtype=np.float32)
    idx = np.arange(128)
    masktri = (idx[:, None] <= idx[None, :]).astype(np.float32)

    in_maps = []
    for c in range(NCORE):
        lo = c * T
        if c == 0:
            xts = np.concatenate(
                [np.zeros((D, HALO), np.float32), xT[:, 0:T]], axis=1)
        else:
            xts = xT[:, lo - HALO: lo + T]
        selbc = np.tile((np.arange(NE) < c).astype(np.float32)[None, :],
                        (NS, 1))
        gmaskt = np.zeros((NE, NE), np.float32)
        for j in range(NE):
            for k in range(NE):
                if j + 1 <= k <= c - 1:
                    gmaskt[k, j] = 1.0
        in_maps.append(dict(
            xt=np.ascontiguousarray(xts), ipwt_x=ipwt_x, ipwt_z=ipwt_z,
            xpwt=xpwt, opwt=opwt, w12t=w12t, w3t=w3t, gwt=gwt,
            convw_p=convw_p, convb_p=convb_p, drow=drow, alog=alog,
            ones2d=ones2d, identf_h=identf, masktri_h=masktri,
            selbc_h=np.ascontiguousarray(selbc), gmaskt_h=gmaskt))

    if "nc" not in _NC_CACHE:
        _NC_CACHE["nc"] = build_nc()
    nc = _NC_CACHE["nc"]
    res = run_bass_kernel_spmd(nc, in_maps, core_ids=list(range(NCORE)))
    if DEBUG:
        kernel.dbg = res.results
    outs = [np.asarray(r["outt"], np.float32).T for r in res.results]
    return np.concatenate(outs, axis=0)[None]


